# version 28
# baseline (speedup 1.0000x reference)
import numpy as np
from contextlib import ExitStack

import concourse.bass as bass
import concourse.mybir as mybir
from concourse.bass_utils import run_bass_kernel_spmd

F32 = mybir.dt.float32
I32 = mybir.dt.int32
AF = mybir.ActivationFunctionType
ALU = mybir.AluOpType

B, T, D, HID, RK = 2, 8192, 1024, 4096, 16
NCORE = 8
TC = T * B // NCORE
TAU = 256
NT = TC // TAU
NSUB = TAU // 128
DC = D // 128
HC = HID // 128
EPS = 1e-5
INV_SCALE = float(1.0 / np.float64(np.float32(np.sqrt(np.float64(1024 * 2048)))))
INV_D = float(1.0 / 1024.0)


def _slab_kxo(w):
    K, O = w.shape
    a = np.ascontiguousarray(w).reshape(K // 128, 128, O // 128, 128)
    return np.ascontiguousarray(a.transpose(2, 1, 0, 3).reshape(O // 128, 128, K))


def _fm_bias(b):
    return np.ascontiguousarray(np.asarray(b).reshape(-1, 128).T)


def build_nc(dev=False, phases=3, qprep=True, nt=NT, upto=99):
    nc = bass.Bass("TRN2", target_bir_lowering=False, debug=False,
                   detect_race_conditions=False)

    def din(name, shape):
        return nc.dram_tensor(name, shape, F32, kind="ExternalInput").ap()

    x_in = din("x", [TC, D])
    w_attn = {n: din("w_" + n, [DC, 128, D]) for n in ("1q", "2q", "1k", "2k", "1v", "2v")}
    bias_fm = {n: din("b_" + n, [128, DC]) for n in ("1q", "1k", "1v", "2q", "2k")}
    b2v_bc = din("b2v_bc", [128, D])
    lnw_bc = din("lnw_bc", [128, D])
    lnb_bc = din("lnb_bc", [128, D])
    bproj_bc = din("bproj_bc", [128, D])
    bfc_fm = din("bfc_fm", [128, HC])
    wfcT_r = din("wfcT_r", [HC, 128, D])
    wprojT_r = din("wprojT_r", [HC, 128, D])
    afcT = din("afcT", [128, DC * RK])
    bfcT = din("bfcT", [RK, HID])
    aprojT = din("aprojT", [128, HC * RK])
    bprojT = din("bprojT", [RK, D])
    masks = din("masks", [128, NSUB * TAU])
    pmask = din("pmask", [128, NCORE])
    ident = din("ident", [128, 128])

    y_out = nc.dram_tensor("y", [TC, D], F32, kind="ExternalOutput").ap()

    ikind = "ExternalOutput" if dev else "Internal"
    q_st = nc.dram_tensor("q_st", [NT, 128, DC * TAU], F32, kind=ikind).ap()
    ol_st = nc.dram_tensor("ol_st", [NT, 128, NSUB * D], F32, kind=ikind).ap()
    BF16 = mybir.dt.bfloat16
    wfcq = nc.dram_tensor("wfcq", [HC, 128, 2 * D], BF16, kind=ikind).ap()
    wprojq = nc.dram_tensor("wprojq", [HC, 128, 2 * D], BF16, kind=ikind).ap()
    cc_in = nc.dram_tensor("cc_in", [128, DC * D], F32, kind="Internal").ap()
    cc_out = nc.dram_tensor("cc_out", [NCORE * 128, DC * D], F32, kind="Internal", addr_space="Shared").ap()
    scal_dr = nc.dram_tensor("scal_dr", [1, 8], F32, kind="Internal").ap()
    if dev:
        dbgS = nc.dram_tensor("dbgS", [128, DC * D], F32, kind="ExternalOutput").ap()
        dbgS0 = nc.dram_tensor("dbgS0", [128, DC * D], F32, kind="ExternalOutput").ap()
        dbgH = nc.dram_tensor("dbgH", [128, HC * TAU], F32, kind="ExternalOutput").ap()
        dbgX2 = nc.dram_tensor("dbgX2", [128, NSUB * D], F32, kind="ExternalOutput").ap()

    with ExitStack() as ctx:
        sb = lambda name, shape, dt=F32: ctx.enter_context(nc.sbuf_tensor(name, shape, dt))
        ident_sb = sb("ident_sb", [128, 128])
        masks_sb = sb("masks_sb", [128, NSUB * TAU])
        pmask_sb = sb("pmask_sb", [128, NCORE])
        bias_sb = {n: sb("bias_" + n, [128, DC]) for n in bias_fm}
        b2v_sb = sb("b2v_sb", [128, D])
        lnw_sb = sb("lnw_sb", [128, D])
        lnb_sb = sb("lnb_sb", [128, D])
        bproj_sb = sb("bproj_sb", [128, D])
        bfc_sb = sb("bfc_sb", [128, HC])
        afcT_sb = sb("afcT_sb", [128, DC * RK])

        aprojT_sb = sb("aprojT_sb", [128, HC * RK])
        bprojT_sb = sb("bprojT_sb", [RK, D])
        S_sb = sb("S_sb", [128, DC * D])
        stats = sb("stats", [128, 32])
        sq_sc = sb("sq_sc", [128, D])
        qmax = sb("qmax", [128, 4])
        sbc = sb("sbc", [128, 4])
        xt = sb("xt", [128, NSUB * D])
        xn_tm = sb("xn_tm", [128, NSUB * D])
        bigbuf = sb("bigbuf", [128, HC * TAU])
        h1 = sb("h1", [128, DC * TAU])
        qf = sb("qf", [128, DC * TAU])
        kf = sb("kf", [128, DC * TAU])
        ktm = sb("ktm", [128, NSUB * D])
        vtm = sb("vtm", [128, NSUB * D])
        ol = sb("ol", [128, NSUB * D])
        ws = [sb(f"ws{i}", [128, D]) for i in range(4)]
        wsb = [sb(f"wsb{i}", [128, 2 * D], mybir.dt.bfloat16) for i in range(2)]
        qi32 = sb("qi32", [128, D], I32)
        bfa = sb("bfa", [128, 2 * DC * TAU], mybir.dt.bfloat16)
        qout = [sb(f"qout{i}", [128, 2 * D], mybir.dt.bfloat16) for i in range(2)]
        lora1 = sb("lora1", [RK, TAU])
        lora2 = sb("lora2", [RK, TAU])
        xn_fm = bigbuf[:, 0:DC * TAU]
        qraw = [bigbuf[:, DC * TAU + 1024 * i:DC * TAU + 1024 * (i + 1)]
                for i in range(2)]
        qd = bigbuf[:, 4096:5120]
        qr = bigbuf[:, 5120:6144]
        sc_sb = bigbuf[:, 6144:6144 + NSUB * TAU]
        h_fm = bigbuf[:, 0:HC * TAU]

        ps = lambda name, shape: ctx.enter_context(nc.psum_tensor(name, shape, F32))
        p_tr = ps("p_tr", [128, 256])
        p_a = ps("p_a", [128, 256])
        p_b = ps("p_b", [128, 256])
        p_big = [ps("p_big0", [128, 1024]), ps("p_big1", [128, 1024])]
        p_l = ps("p_l", [128, 256])

        SEM = {}
        for s_ in ("ldc", "ldx", "ldb", "lw0", "lw1", "lw2", "lw3",
                   "gq0", "gq1", "gst", "gstash", "gcc", "gdbg",
                   "pe", "ve", "ac", "cc", "st"):
            SEM[s_] = ctx.enter_context(nc.semaphore("sem_" + s_))
        block = ctx.enter_context(nc.Block())

        Q = {"sync": [], "vector": [], "scalar": [], "tensor": [], "gpsimd": []}
        C = {k: 0 for k in SEM}

        def W_(eng, sem, val):
            if val is None or val <= 0:
                return
            Q[eng].append(lambda e, s=sem, v=val: e.wait_ge(SEM[s], v))

        def OP(eng, fn, inc=None):
            if inc is None:
                Q[eng].append(fn)
                return None
            sem, k = inc
            C[sem] += k
            Q[eng].append(lambda e, f=fn, s=sem, kk=k: f(e).then_inc(SEM[s], kk))
            return C[sem]

        def PEG():
            C["pe"] += 1
            last = Q["tensor"].pop()
            Q["tensor"].append(lambda e, f=last: f(e).then_inc(SEM["pe"], 1))
            return C["pe"]

        def VC(fn):
            n = OP("vector", fn, ("ve", 1))
            W_("vector", "ve", n)
            return n

        def DMA(eng, out_ap, in_ap, sem):
            return OP(eng, lambda e, o=out_ap, i=in_ap: e.dma_start(out=o, in_=i),
                      (sem, 16))

        evac = {"p_a": ("ve", 0), "p_b": ("ve", 0), "p_tr": ("ve", 0),
                "p_big0": ("ve", 0), "p_big1": ("ve", 0), "p_l": ("ac", 0)}

        def wait_bank(eng, name):
            s_, v = evac[name]
            W_(eng, s_, v)

        xt3 = xt[:].rearrange("p (s d) -> p s d", s=NSUB)
        xn3 = xn_tm[:].rearrange("p (s d) -> p s d", s=NSUB)
        xnf3 = xn_fm.rearrange("p (c t) -> p c t", c=DC)
        h13 = h1[:].rearrange("p (c t) -> p c t", c=DC)
        h1s3 = h1[:].rearrange("p (s d) -> p s d", s=NSUB)
        qf3 = qf[:].rearrange("p (c t) -> p c t", c=DC)
        kf3 = kf[:].rearrange("p (c t) -> p c t", c=DC)
        kfs3 = kf[:].rearrange("p (s d) -> p s d", s=NSUB)
        ktm3 = ktm[:].rearrange("p (a d) -> p a d", a=NSUB)
        vtm3 = vtm[:].rearrange("p (a d) -> p a d", a=NSUB)
        ol3 = ol[:].rearrange("p (a d) -> p a d", a=NSUB)
        S3 = S_sb[:].rearrange("p (c d) -> p c d", c=DC)
        xn2h3 = bfa[:, 0:DC * TAU].rearrange("p (c t) -> p c t", c=DC)
        xn2l3 = bfa[:, DC * TAU:2 * DC * TAU].rearrange("p (c t) -> p c t", c=DC)
        hf3 = h_fm.rearrange("p (c t) -> p c t", c=HC)
        hh3 = xn2h3
        hl3 = xn2l3
        bfcT_sb = ktm[0:RK, 0:NSUB * D]
        msk3 = masks_sb[:].rearrange("p (a t) -> p a t", a=NSUB)

        DMA("sync", ident_sb[:], ident, "ldc")
        DMA("sync", masks_sb[:], masks, "ldc")
        DMA("sync", pmask_sb[:], pmask, "ldc")
        for n in bias_sb:
            DMA("sync", bias_sb[n][:], bias_fm[n], "ldc")
        DMA("sync", b2v_sb[:], b2v_bc, "ldc")
        DMA("sync", lnw_sb[:], lnw_bc, "ldc")
        DMA("sync", lnb_sb[:], lnb_bc, "ldc")
        DMA("sync", bproj_sb[:], bproj_bc, "ldc")
        DMA("sync", bfc_sb[:], bfc_fm, "ldc")
        DMA("sync", afcT_sb[:], afcT, "ldc")
        DMA("sync", aprojT_sb[:], aprojT, "ldc")
        DMA("sync", bprojT_sb[:], bprojT, "ldc")
        ld_consts = C["ldc"]
        OP("vector", lambda e: e.memset(S_sb[:], 0.0))
        ve_S_ready = OP("vector", lambda e: e.memset(qmax[:, 0:2], 0.0), ("ve", 1))

        def q_slab_in(i):
            return wfcT_r[i] if i < HC else wprojT_r[i - HC]

        def q_slab_out(i):
            return wfcq[i] if i < HC else wprojq[i - HC]

        qstate = {"ve_prev": 0, "gd_load": [0] * 64, "ac9": 0, "st_gd": 0,
                  "st_prev": 0, "sbc_gd": 0, "sq_guard": 0}

        def emit_qmax_slab(i):
            buf = qraw[i % 2]
            W_("gpsimd", "ve", qstate["ve_prev"])
            qstate["gd_load"][i] = DMA("gpsimd", buf, q_slab_in(i), "gq%d" % (i % 2))
            W_("vector", "gq%d" % (i % 2), qstate["gd_load"][i])
            if i == 0:
                W_("vector", "ve", ve_S_ready)
            m = 0 if i < HC else 1
            VC(lambda e, b=buf: e.reduce_max(
                out=qmax[:, 2:3], in_=b, axis=mybir.AxisListType.X,
                apply_absolute_value=True))
            qstate["ve_prev"] = VC(lambda e, m=m: e.tensor_max(
                out=qmax[:, m:m + 1], in0=qmax[:, m:m + 1], in1=qmax[:, 2:3]))

        def emit_qscalars():
            for m in range(2):
                W_("vector", "gcc", C["gcc"])
                W_("vector", "ac", C["ac"])
                VC(lambda e: e.memset(sq_sc[:, 0:128], 0.0))
                vq = VC(lambda e, m=m: e.tensor_copy(
                    out=sq_sc[:, 0:1], in_=qmax[:, m:m + 1]))
                W_("tensor", "ve", vq)
                W_("tensor", "ldc", ld_consts)
                pe_n = OP("tensor", lambda e: e.transpose(
                    out=p_tr[:, 0:128], in_=sq_sc[:, 0:128], identity=ident_sb[:]),
                    ("pe", 1))
                qstate["sq_guard"] = pe_n
                W_("vector", "pe", pe_n)
                VC(lambda e: e.reduce_max(
                    out=stats[0:1, 16:17], in_=p_tr[0:1, 0:128],
                    axis=mybir.AxisListType.X))
                VC(lambda e: e.memset(stats[0:1, 17:18], 127.0))
                VC(lambda e: e.reciprocal(
                    out=stats[0:1, 18:19], in_=stats[0:1, 17:18]))
                VC(lambda e: e.tensor_mul(
                    out=stats[0:1, 19:20], in0=stats[0:1, 16:17],
                    in1=stats[0:1, 18:19]))
                vq2 = VC(lambda e: e.reciprocal(
                    out=stats[0:1, 20:21], in_=stats[0:1, 19:20]))
                W_("gpsimd", "ve", vq2)
                DMA("gpsimd", scal_dr[0:1, 2 * m:2 * m + 1], stats[0:1, 19:20], "gcc")
                g1 = DMA("gpsimd", scal_dr[0:1, 2 * m + 1:2 * m + 2],
                         stats[0:1, 20:21], "gcc")
                W_("gpsimd", "gcc", g1)
                DMA("gpsimd", sbc[:, 2 * m:2 * m + 1],
                    scal_dr[0:1, 2 * m:2 * m + 1].broadcast_to([128, 1]), "gcc")
                qstate["sbc_gd"] = DMA(
                    "gpsimd", sbc[:, 2 * m + 1:2 * m + 2],
                    scal_dr[0:1, 2 * m + 1:2 * m + 2].broadcast_to([128, 1]), "gcc")

        def emit_qquant_slab(i):
            buf = qraw[i % 2]
            qrx = qr
            m = 0 if i < HC else 1
            W_("gpsimd", "ve", qstate["ve_prev"])
            ld_i = DMA("gpsimd", buf, q_slab_in(i), "gq%d" % (i % 2))
            W_("vector", "gq%d" % (i % 2), ld_i)
            W_("vector", "gcc", qstate["sbc_gd"])
            W_("vector", "gst", qstate["st_prev"])
            OP("vector", lambda e, b=buf, m=m: e.tensor_scalar(
                out=qd, in0=b, scalar1=sbc[:, 2 * m + 1:2 * m + 2], scalar2=None,
                op0=ALU.mult))
            OP("vector", lambda e: e.tensor_copy(out=qi32[:], in_=qd))
            OP("vector", lambda e, q=qrx: e.tensor_copy(out=q, in_=qi32[:]))
            OP("vector", lambda e, b=buf, q=qrx: e.tensor_sub(out=b, in0=qd, in1=q))
            OP("vector", lambda e, b=buf, q=qrx: e.scalar_tensor_tensor(
                out=qd, in0=b, scalar=0.5, in1=q,
                op0=ALU.is_ge, op1=ALU.add))
            OP("vector", lambda e, b=buf, q=qrx: e.tensor_scalar(
                out=q, in0=b, scalar1=-0.5, scalar2=None, op0=ALU.is_lt))
            OP("vector", lambda e, b=buf, q=qrx: e.tensor_sub(out=b, in0=qd, in1=q))
            OP("vector", lambda e, b=buf: e.tensor_scalar(
                out=qd, in0=b, scalar1=-128.0, scalar2=127.0,
                op0=ALU.max, op1=ALU.min))
            qo = qout[i % 2]
            OP("vector", lambda e, m=m: e.tensor_scalar(
                out=qr, in0=qd, scalar1=sbc[:, 2 * m:2 * m + 1], scalar2=None,
                op0=ALU.mult))
            OP("vector", lambda e, q=qo: e.tensor_copy(out=q[:, 0:D], in_=qr))
            qstate["ve_prev"] = OP("vector", lambda e, q=qo: e.tensor_sub(
                out=q[:, D:2 * D], in0=qr, in1=q[:, 0:D]), ("ve", 1))
            W_("gpsimd", "ve", qstate["ve_prev"])
            qstate["st_prev"] = qstate["st_gd"]
            qstate["st_gd"] = DMA("gpsimd", q_slab_out(i), qo[:], "gst")

        qsched = {t: [] for t in range(NT)}
        for i in range(64 if qprep else 0):
            qsched[i // 32].append(("max", i))
        quant_per_tile = [11, 11, 11, 11, 10, 10] if qprep else []
        qi = 0
        for ti, n_ in enumerate(quant_per_tile):
            for _ in range(n_):
                qsched[2 + ti].append(("quant", qi))
                qi += 1
        assert qi == 64 or not qprep

        wnames = ["1q", "2q", "1k", "2k", "1v", "2v"]
        ws_release = [0, 0, 0, 0]
        ws_loaded = {}
        prev = {"ve_ln": 0, "ve_S": ve_S_ready, "gd_stash": 0}

        def tile_rows(t):
            return x_in[t * TAU:(t + 1) * TAU, :].rearrange("(s p) d -> p s d", p=128)

        ph1 = {}
        ph_ln = {}

        def emit_ln(t):
            W_("sync", "ve", prev["ve_ln"])
            ld_x = DMA("sync", xt3, tile_rows(t), "ldx")
            ph_ln[(t, "ldx")] = ld_x
            W_("vector", "gstash", prev["gd_stash"])
            W_("scalar", "gstash", prev["gd_stash"])
            W_("scalar", "pe", qstate["sq_guard"])
            for s_ in range(NSUB):
                W_("vector", "ldx", ld_x)
                VC(lambda e, s=s_: e.reduce_sum(
                    out=stats[:, 4 * s:4 * s + 1], in_=xt3[:, s, :],
                    axis=mybir.AxisListType.X))
                W_("scalar", "ldx", ld_x)
                ac_sq = OP("scalar", lambda e, s=s_: e.activation(
                    out=sq_sc[:], in_=xt3[:, s, :], func=AF.Square,
                    accum_out=stats[:, 4 * s + 1:4 * s + 2]), ("ac", 1))
                W_("vector", "ac", ac_sq)
                VC(lambda e, s=s_: e.tensor_scalar(
                    out=stats[:, 4 * s:4 * s + 1], in0=stats[:, 4 * s:4 * s + 1],
                    scalar1=INV_D, scalar2=None, op0=ALU.mult))
                VC(lambda e, s=s_: e.tensor_scalar(
                    out=stats[:, 4 * s + 1:4 * s + 2],
                    in0=stats[:, 4 * s + 1:4 * s + 2],
                    scalar1=INV_D, scalar2=None, op0=ALU.mult))
                VC(lambda e, s=s_: e.tensor_mul(
                    out=stats[:, 4 * s + 2:4 * s + 3], in0=stats[:, 4 * s:4 * s + 1],
                    in1=stats[:, 4 * s:4 * s + 1]))
                vv = VC(lambda e, s=s_: e.scalar_tensor_tensor(
                    out=stats[:, 4 * s + 2:4 * s + 3],
                    in0=stats[:, 4 * s + 1:4 * s + 2], scalar=EPS,
                    in1=stats[:, 4 * s + 2:4 * s + 3],
                    op0=ALU.add, op1=ALU.subtract))
                W_("scalar", "ve", vv)
                ac_sd = OP("scalar", lambda e, s=s_: e.activation(
                    out=stats[:, 4 * s + 3:4 * s + 4],
                    in_=stats[:, 4 * s + 2:4 * s + 3], func=AF.Sqrt), ("ac", 1))
                W_("vector", "ac", ac_sd)
                VC(lambda e, s=s_: e.reciprocal(
                    out=stats[:, 4 * s + 2:4 * s + 3],
                    in_=stats[:, 4 * s + 3:4 * s + 4]))
                OP("vector", lambda e, s=s_: e.tensor_scalar(
                    out=xn3[:, s, :], in0=xt3[:, s, :],
                    scalar1=stats[:, 4 * s:4 * s + 1],
                    scalar2=stats[:, 4 * s + 2:4 * s + 3],
                    op0=ALU.subtract, op1=ALU.mult), ("ve", 1))
            prev["ve_ln"] = C["ve"]
            ph_ln[t] = C["ve"]

        def emit_tr(t):
            ve_xn = ph_ln[t]
            ld_x = ph_ln[(t, "ldx")]
            for dc in range(DC):
                pb, pname = ((p_a, "p_a"), (p_b, "p_b"))[dc % 2]
                W_("tensor", "ve", ve_xn)
                wait_bank("tensor", pname)
                W_("tensor", "ldc", ld_consts)
                for s_ in range(NSUB):
                    OP("tensor", lambda e, s=s_, dc=dc, pb=pb: e.transpose(
                        out=pb[:, s * 128:s * 128 + 128],
                        in_=xn3[:, s, dc * 128:dc * 128 + 128],
                        identity=ident_sb[:]))
                pe_n = PEG()
                W_("vector", "pe", pe_n)
                evac[pname] = ("ve", OP("vector", lambda e, dc=dc, pb=pb: e.tensor_copy(
                    out=xnf3[:, dc, :], in_=pb[:, 0:TAU]), ("ve", 1)))
            ph1[t] = (ld_x, C["ve"])

        def emit_phase2(t, mid_hook=None):
            ld_x, ve_xnf = ph1[t]
            wsi_cnt = {"i": 0}

            def next_wsi():
                v = wsi_cnt["i"] % 4
                wsi_cnt["i"] += 1
                return v

            def load_wslab(mat, slab, wsi):
                W_("sync", "pe", ws_release[wsi])
                ws_loaded[(t, mat, slab)] = (
                    "lw%d" % wsi,
                    DMA("sync", ws[wsi][:], w_attn[mat][slab], "lw%d" % wsi))

            def mlp_h1(mat, bias_name):
                for hc_ in range(DC):
                    wsi = next_wsi()
                    load_wslab(mat, hc_, wsi)
                    pb, pname = ((p_a, "p_a"), (p_b, "p_b"))[hc_ % 2]
                    W_("tensor", *ws_loaded[(t, mat, hc_)])
                    W_("tensor", "ve", ve_xnf)
                    wait_bank("tensor", pname)
                    for kc in range(DC):
                        OP("tensor", lambda e, kc=kc, pb=pb, wsi=wsi: e.matmul(
                            out=pb[:, 0:TAU],
                            lhsT=ws[wsi][:, kc * 128:kc * 128 + 128],
                            rhs=xnf3[:, kc, :],
                            start=(kc == 0), stop=(kc == DC - 1)))
                    pe_n = PEG()
                    ws_release[wsi] = pe_n
                    W_("scalar", "pe", pe_n)
                    evac[pname] = ("ac", OP("scalar", lambda e, hc=hc_, pb=pb,
                                            bn=bias_name: e.activation(
                        out=h13[:, hc, :], in_=pb[:, 0:TAU], func=AF.Gelu,
                        bias=bias_sb[bn][:, hc:hc + 1]), ("ac", 1)))
                return C["ac"]

            def mlp_out_fm(mat, bias_name, dst3, ac_h1):
                last = 0
                for oc in range(DC):
                    wsi = next_wsi()
                    load_wslab(mat, oc, wsi)
                    pb, pname = ((p_a, "p_a"), (p_b, "p_b"))[oc % 2]
                    W_("tensor", *ws_loaded[(t, mat, oc)])
                    W_("tensor", "ac", ac_h1)
                    wait_bank("tensor", pname)
                    for hc_ in range(DC):
                        OP("tensor", lambda e, hc=hc_, pb=pb, wsi=wsi: e.matmul(
                            out=pb[:, 0:TAU],
                            lhsT=ws[wsi][:, hc * 128:hc * 128 + 128],
                            rhs=h13[:, hc, :],
                            start=(hc == 0), stop=(hc == DC - 1)))
                    pe_n = PEG()
                    ws_release[wsi] = pe_n
                    W_("scalar", "pe", pe_n)
                    last = OP("scalar", lambda e, oc=oc, pb=pb, bn=bias_name,
                              d3=dst3: e.activation(
                        out=d3[:, oc, :], in_=pb[:, 0:TAU], func=AF.Identity,
                        bias=bias_sb[bn][:, oc:oc + 1]), ("ac", 1))
                    evac[pname] = ("ac", last)
                return last

            def mlp_out_vtm(ac_h1):
                wait_bank("tensor", "p_big0")
                wait_bank("tensor", "p_big1")
                W_("tensor", "ac", ac_h1)
                pe_n = 0
                for oc in range(DC):
                    wsi = next_wsi()
                    load_wslab("2v", oc, wsi)
                    W_("tensor", *ws_loaded[(t, "2v", oc)])
                    for a in range(NSUB):
                        for hc_ in range(DC):
                            OP("tensor", lambda e, hc=hc_, oc=oc, a=a, wsi=wsi:
                               e.matmul(
                                out=p_big[a][:, oc * 128:oc * 128 + 128],
                                lhsT=h13[:, hc, a * 128:a * 128 + 128],
                                rhs=ws[wsi][:, hc * 128:hc * 128 + 128],
                                start=(hc == 0), stop=(hc == DC - 1)))
                    pe_n = PEG()
                    ws_release[wsi] = pe_n
                for a in range(NSUB):
                    W_("vector", "pe", pe_n)
                    evac["p_big%d" % a] = ("ve", OP(
                        "vector", lambda e, a=a: e.tensor_add(
                            out=vtm3[:, a, :], in0=p_big[a][:], in1=b2v_sb[:]),
                        ("ve", 1)))

            if upto < 3:
                return
            ac_h1q = mlp_h1("1q", "1q")
            if mid_hook is not None:
                mid_hook("a")
            qf_done = mlp_out_fm("2q", "2q", qf3, ac_h1q)
            if mid_hook is not None:
                mid_hook("b")
            if upto < 4:
                return
            ac_h1k = mlp_h1("1k", "1k")
            if mid_hook is not None:
                mid_hook("c")
            kf_done = mlp_out_fm("2k", "2k", kf3, ac_h1k)
            if mid_hook is not None:
                mid_hook("d")
            if upto < 5:
                return
            ac_h1v = mlp_h1("1v", "1v")
            if mid_hook is not None:
                mid_hook("e")
            mlp_out_vtm(ac_h1v)
            ve_vtm = C["ve"]

            if upto < 6:
                return
            for a in range(NSUB):
                pbn = "p_big%d" % (a % 2)
                W_("tensor", "ac", kf_done)
                wait_bank("tensor", pbn)
                for dc in range(DC):
                    OP("tensor", lambda e, dc=dc, a=a: e.transpose(
                        out=p_big[a % 2][:, dc * 128:dc * 128 + 128],
                        in_=kf3[:, dc, a * 128:a * 128 + 128],
                        identity=ident_sb[:]))
                pe_n = PEG()
                W_("vector", "pe", pe_n)
                evac[pbn] = ("ve", OP("vector", lambda e, a=a: e.tensor_copy(
                    out=ktm3[:, a, :], in_=p_big[a % 2][:]), ("ve", 1)))
            ve_ktm = C["ve"]

            if upto < 7:
                return
            for ac_ in range(NSUB):
                pb, pname = ((p_a, "p_a"), (p_b, "p_b"))[ac_ % 2]
                W_("tensor", "ac", max(qf_done, kf_done))
                wait_bank("tensor", pname)
                for kc in range(DC):
                    OP("tensor", lambda e, kc=kc, ac=ac_, pb=pb: e.matmul(
                        out=pb[:, 0:TAU],
                        lhsT=kf3[:, kc, ac * 128:ac * 128 + 128],
                        rhs=qf3[:, kc, :],
                        start=(kc == 0), stop=(kc == DC - 1)))
                pe_n = PEG()
                W_("vector", "pe", pe_n)
                evac[pname] = ("ve", OP("vector", lambda e, ac=ac_, pb=pb:
                                        e.scalar_tensor_tensor(
                    out=sc_sb[:, ac * TAU:(ac + 1) * TAU], in0=pb[:, 0:TAU],
                    scalar=INV_SCALE, in1=msk3[:, ac, :],
                    op0=ALU.mult, op1=ALU.mult), ("ve", 1)))
            ve_sT = C["ve"]

            if upto < 8:
                return
            for a in range(NSUB):
                pbn = "p_big%d" % a
                W_("tensor", "ve", max(ve_sT, ve_vtm, prev["ve_S"]))
                wait_bank("tensor", pbn)
                for d1c in range(DC):
                    for dh in range(2):
                        OP("tensor", lambda e, d1c=d1c, dh=dh, a=a: e.matmul(
                            out=p_big[a][:, dh * 512:dh * 512 + 512],
                            lhsT=qf3[:, d1c, a * 128:a * 128 + 128],
                            rhs=S3[:, d1c, dh * 512:dh * 512 + 512],
                            start=(d1c == 0), stop=False))
                for ac_ in range(NSUB):
                    for dh in range(2):
                        OP("tensor", lambda e, ac=ac_, dh=dh, a=a: e.matmul(
                            out=p_big[a][:, dh * 512:dh * 512 + 512],
                            lhsT=sc_sb[:, ac * TAU + a * 128:ac * TAU + a * 128 + 128],
                            rhs=vtm3[:, ac, dh * 512:dh * 512 + 512],
                            start=False, stop=(ac == NSUB - 1)))
                pe_n = PEG()
                W_("vector", "pe", pe_n)
                evac[pbn] = ("ve", OP("vector", lambda e, a=a: e.tensor_copy(
                    out=ol3[:, a, :], in_=p_big[a][:]), ("ve", 1)))
            ve_ol = C["ve"]
            pe_qS_done = C["pe"]

            if upto < 9:
                return
            for d1c in range(DC):
                pbn = "p_big%d" % (d1c % 2)
                wait_bank("tensor", pbn)
                W_("tensor", "ve", ve_ktm)
                for a in range(NSUB):
                    for dh in range(2):
                        OP("tensor", lambda e, a=a, dh=dh, d1c=d1c: e.matmul(
                            out=p_big[d1c % 2][:, dh * 512:dh * 512 + 512],
                            lhsT=ktm3[:, a, d1c * 128:d1c * 128 + 128],
                            rhs=vtm3[:, a, dh * 512:dh * 512 + 512],
                            start=(a == 0), stop=(a == NSUB - 1)))
                pe_n = PEG()
                W_("vector", "pe", pe_n)
                evac[pbn] = ("ve", OP("vector", lambda e, d1c=d1c: e.scalar_tensor_tensor(
                    out=S3[:, d1c, :], in0=p_big[d1c % 2][:], scalar=INV_SCALE,
                    in1=S3[:, d1c, :], op0=ALU.mult, op1=ALU.add), ("ve", 1)))
            prev["ve_S"] = C["ve"]

            if upto < 10:
                return
            W_("gpsimd", "gstash", prev["gd_stash"])
            W_("gpsimd", "ve", ve_ol)
            DMA("gpsimd", ol_st[t], ol[:], "gstash")
            W_("gpsimd", "ac", qf_done)
            W_("gpsimd", "pe", pe_qS_done)
            DMA("gpsimd", q_st[t], qf[:], "gstash")
            prev["gd_stash"] = C["gstash"]

            return

        def hook_fn(t, stage):
            if stage == "b":
                if t + 1 < nt:
                    emit_ln(t + 1)
                return
            items = qsched.get(t, [])
            k4 = (len(items) + 3) // 4 if items else 0
            idx = {"a": 0, "c": 1, "d": 2, "e": 3}[stage]
            for kind, i in items[idx * k4:(idx + 1) * k4]:
                if kind == "max":
                    emit_qmax_slab(i)
                else:
                    emit_qquant_slab(i)

        emit_ln(0)
        emit_tr(0)
        for t in range(nt):
            emit_phase2(t, lambda stage, tt=t: hook_fn(tt, stage))
            if t + 1 < nt:
                emit_tr(t + 1)
            if t == 1 and qprep:
                emit_qscalars()

        pe_sweepA = C["pe"]

        if phases == 1:
            if dev:
                W_("gpsimd", "ve", prev["ve_S"])
                DMA("gpsimd", dbgS, S_sb[:], "gdbg")
                W_("gpsimd", "gdbg", C["gdbg"])
            W_("gpsimd", "gstash", C["gstash"])
            W_("gpsimd", "gst", C["gst"])
        phase_c_on = phases >= 2
        ve_S0 = 0
        if phase_c_on:
          W_("gpsimd", "ve", prev["ve_S"])
          gcc_ccin = DMA("gpsimd", cc_in, S_sb[:], "gcc")
          W_("gpsimd", "gcc", C["gcc"])
          OP("gpsimd", lambda e: e.collective_compute(
              "AllGather", ALU.bypass,
              replica_groups=[[0, 1, 2, 3, 4, 5, 6, 7]],
              ins=[cc_in], outs=[cc_out]), ("cc", 1))
          W_("gpsimd", "cc", C["cc"])
          if dev:
              DMA("gpsimd", dbgS, S_sb[:], "gdbg")
          W_("gpsimd", "pe", pe_sweepA)
          W_("gpsimd", "gstash", C["gstash"])
          W_("gpsimd", "gst", C["gst"])
          W_("vector", "gcc", gcc_ccin)
          OP("vector", lambda e: e.memset(S_sb[:], 0.0), ("ve", 1))
          W_("sync", "pe", pe_sweepA)
          W_("sync", "gst", C["gst"])
          W_("sync", "gstash", C["gstash"])
          W_("sync", "cc", C["cc"])
          HB = DC * D // 2
          ve_half = {0: C["ve"], 1: C["ve"]}
          for j in range(NCORE):
              for h in range(2):
                  qn, sn = (("gpsimd", "gcc"), ("sync", "ldx"))[h]
                  W_(qn, "ve", ve_half[h])
                  gd_j = DMA(qn, bigbuf[:, h * HB:(h + 1) * HB],
                             cc_out[j * 128:(j + 1) * 128, h * HB:(h + 1) * HB],
                             sn)
                  W_("vector", sn, gd_j)
                  ve_half[h] = OP("vector", lambda e, j=j, h=h: e.scalar_tensor_tensor(
                      out=S_sb[:, h * HB:(h + 1) * HB],
                      in0=bigbuf[:, h * HB:(h + 1) * HB],
                      scalar=pmask_sb[:, j:j + 1],
                      in1=S_sb[:, h * HB:(h + 1) * HB],
                      op0=ALU.mult, op1=ALU.add), ("ve", 1))
          ve_S0 = C["ve"]
          if dev:
              W_("gpsimd", "ve", ve_S0)
              DMA("gpsimd", dbgS0, S_sb[:], "gdbg")
          if phases == 2:
              W_("gpsimd", "gcc", C["gcc"])
              W_("gpsimd", "gst", C["gst"])
              W_("gpsimd", "gstash", C["gstash"])
              if dev:
                  W_("gpsimd", "gdbg", C["gdbg"])

        xb3, qb3, olb3, ofull3 = xt3, qf3, ol3, xn3
        x2v = kfs3
        xn2tm3 = vtm3
        xn2fm3 = qf3
        hat3 = h1s3
        t1_3 = ktm3
        fin3 = h1s3
        def stp(base, k):
            return stats[:, base:base + 8].rearrange(
                "p (s c) -> p s c", c=4)[:, :, k:k + 1]

        def emit_ln1_stats():
            o = 8
            for s_ in range(NSUB):
                VC(lambda e, s=s_, o=o: e.reduce_sum(
                    out=stats[:, o + 4 * s:o + 4 * s + 1], in_=ofull3[:, s, :],
                    axis=mybir.AxisListType.X))
            W_("scalar", "ve", C["ve"])
            ac_sq = 0
            for s_ in range(NSUB):
                ac_sq = OP("scalar", lambda e, s=s_, o=o: e.activation(
                    out=sq_sc[:], in_=ofull3[:, s, :], func=AF.Square,
                    accum_out=stats[:, o + 4 * s + 1:o + 4 * s + 2]), ("ac", 1))
            W_("vector", "ac", ac_sq)
            VC(lambda e, o=o: e.tensor_scalar(
                out=stp(o, 0), in0=stp(o, 0),
                scalar1=INV_D, scalar2=None, op0=ALU.mult))
            VC(lambda e, o=o: e.tensor_scalar(
                out=stp(o, 1), in0=stp(o, 1),
                scalar1=INV_D, scalar2=None, op0=ALU.mult))
            VC(lambda e, o=o: e.tensor_mul(
                out=stp(o, 2), in0=stp(o, 0), in1=stp(o, 0)))
            vv = VC(lambda e, o=o: e.scalar_tensor_tensor(
                out=stp(o, 2), in0=stp(o, 1), scalar=EPS, in1=stp(o, 2),
                op0=ALU.add, op1=ALU.subtract))
            W_("scalar", "ve", vv)
            ac_sd = OP("scalar", lambda e, o=o: e.activation(
                out=stp(o, 3), in_=stp(o, 2), func=AF.Sqrt), ("ac", 1))
            W_("vector", "ac", ac_sd)
            VC(lambda e, o=o: e.reciprocal(out=stp(o, 2), in_=stp(o, 3)))

        bst = {"st_done": 0, "ve_end": ve_S0,
               "wq_rel": [pe_sweepA, pe_sweepA], "ac_gelu": [0] * HC,
               "lora_free": ("ac", 0)}

        for tb in range(NT if phases >= 3 else 0):
            W_("sync", "ve", bst["ve_end"])
            W_("sync", "st", bst["st_done"])
            if tb == 0:
                W_("sync", "gstash", C["gstash"])
                W_("sync", "gst", C["gst"])
                W_("sync", "cc", C["cc"])
                DMA("sync", xb3, tile_rows(tb), "ldb")
                DMA("sync", qf[:], q_st[tb], "ldb")
                ld_b = DMA("sync", ol[:], ol_st[tb], "ldb")
            else:
                ld_b = bst["ld_next"]

            if tb == 0:
                for a in range(NSUB):
                    pbn = "p_big%d" % a
                    W_("tensor", "ldb", ld_b)
                    W_("tensor", "ve", ve_S0)
                    wait_bank("tensor", pbn)
                    for d1c in range(DC):
                        for dh in range(2):
                            OP("tensor", lambda e, d1c=d1c, dh=dh, a=a: e.matmul(
                                out=p_big[a][:, dh * 512:dh * 512 + 512],
                                lhsT=qb3[:, d1c, a * 128:a * 128 + 128],
                                rhs=S3[:, d1c, dh * 512:dh * 512 + 512],
                                start=(d1c == 0), stop=(d1c == DC - 1)))
                    pe_n = PEG()
                    W_("vector", "pe", pe_n)
                    W_("vector", "ldb", ld_b)
                    evac[pbn] = ("ve", OP("vector", lambda e, a=a: e.tensor_add(
                        out=ofull3[:, a, :], in0=p_big[a][:], in1=olb3[:, a, :]),
                        ("ve", 1)))

            if tb == 0:
                emit_ln1_stats()
            o = 8
            W_("vector", "st", bst["st_done"])
            for s_ in range(NSUB):
                OP("vector", lambda e, s=s_, o=o: e.tensor_scalar(
                    out=hat3[:, s, :], in0=ofull3[:, s, :],
                    scalar1=stats[:, o + 4 * s:o + 4 * s + 1],
                    scalar2=stats[:, o + 4 * s + 2:o + 4 * s + 3],
                    op0=ALU.subtract, op1=ALU.mult))
                OP("vector", lambda e, s=s_: e.tensor_mul(
                    out=t1_3[:, s, :], in0=hat3[:, s, :], in1=lnw_sb[:, :]))
                OP("vector", lambda e, s=s_: e.tensor_add(
                    out=hat3[:, s, :], in0=t1_3[:, s, :], in1=lnb_sb[:, :]))
                W_("vector", "ldb", ld_b)
                OP("vector", lambda e, s=s_: e.tensor_add(
                    out=x2v[:, s, :], in0=hat3[:, s, :], in1=xb3[:, s, :]),
                    ("ve", 1))
            o = 16
            for s_ in range(NSUB):
                VC(lambda e, s=s_, o=o: e.reduce_sum(
                    out=stats[:, o + 4 * s:o + 4 * s + 1], in_=x2v[:, s, :],
                    axis=mybir.AxisListType.X))
            W_("scalar", "ve", C["ve"])
            ac_sq = 0
            for s_ in range(NSUB):
                ac_sq = OP("scalar", lambda e, s=s_, o=o: e.activation(
                    out=sq_sc[:], in_=x2v[:, s, :], func=AF.Square,
                    accum_out=stats[:, o + 4 * s + 1:o + 4 * s + 2]), ("ac", 1))
            W_("vector", "ac", ac_sq)
            VC(lambda e, o=o: e.tensor_scalar(
                out=stp(o, 0), in0=stp(o, 0),
                scalar1=INV_D, scalar2=None, op0=ALU.mult))
            VC(lambda e, o=o: e.tensor_scalar(
                out=stp(o, 1), in0=stp(o, 1),
                scalar1=INV_D, scalar2=None, op0=ALU.mult))
            VC(lambda e, o=o: e.tensor_mul(
                out=stp(o, 2), in0=stp(o, 0), in1=stp(o, 0)))
            vv = VC(lambda e, o=o: e.scalar_tensor_tensor(
                out=stp(o, 2), in0=stp(o, 1), scalar=EPS, in1=stp(o, 2),
                op0=ALU.add, op1=ALU.subtract))
            W_("scalar", "ve", vv)
            ac_sd = OP("scalar", lambda e, o=o: e.activation(
                out=stp(o, 3), in_=stp(o, 2), func=AF.Sqrt), ("ac", 1))
            W_("vector", "ac", ac_sd)
            VC(lambda e, o=o: e.reciprocal(out=stp(o, 2), in_=stp(o, 3)))
            for s_ in range(NSUB):
                OP("vector", lambda e, s=s_, o=o: e.tensor_scalar(
                    out=xn2tm3[:, s, :], in0=x2v[:, s, :],
                    scalar1=stats[:, o + 4 * s:o + 4 * s + 1],
                    scalar2=stats[:, o + 4 * s + 2:o + 4 * s + 3],
                    op0=ALU.subtract, op1=ALU.mult), ("ve", 1))
            ve_xn2 = C["ve"]
            if dev and tb == 0:
                W_("gpsimd", "ve", ve_xn2)
                DMA("gpsimd", dbgX2, kf[:], "gdbg")

            for dc in range(DC):
                pb, pname = ((p_a, "p_a"), (p_b, "p_b"))[dc % 2]
                W_("tensor", "ve", ve_xn2)
                wait_bank("tensor", pname)
                for s_ in range(NSUB):
                    OP("tensor", lambda e, s=s_, dc=dc, pb=pb: e.transpose(
                        out=pb[:, s * 128:s * 128 + 128],
                        in_=xn2tm3[:, s, dc * 128:dc * 128 + 128],
                        identity=ident_sb[:]))
                pe_n = PEG()
                W_("scalar", "pe", pe_n)
                ac_hi = OP("scalar", lambda e, dc=dc, pb=pb: e.activation(
                    out=xn2h3[:, dc, :], in_=pb[:, 0:TAU], func=AF.Copy), ("ac", 1))
                W_("vector", "pe", pe_n)
                W_("vector", "ac", ac_hi)
                OP("vector", lambda e, dc=dc, pb=pb: e.tensor_copy(
                    out=xn2fm3[:, dc, :], in_=pb[:, 0:TAU]))
                evac[pname] = ("ve", OP("vector", lambda e, dc=dc, pb=pb:
                                        e.tensor_sub(
                    out=xn2l3[:, dc, :], in0=pb[:, 0:TAU],
                    in1=xn2h3[:, dc, :]), ("ve", 1)))
            ve_xn2f = C["ve"]

            bst["pe_b6start"] = 0
            W_("tensor", "ve", ve_xn2f)
            W_("tensor", "ldc", ld_consts)
            wait_bank("tensor", "p_l")
            for dc in range(DC):
                OP("tensor", lambda e, dc=dc: e.matmul(
                    out=p_l[0:RK, 0:TAU],
                    lhsT=afcT_sb[:, dc * RK:(dc + 1) * RK],
                    rhs=xn2fm3[:, dc, :],
                    start=(dc == 0), stop=(dc == DC - 1)))
            pe_aproj = PEG()
            W_("scalar", "pe", pe_aproj)
            ac_l1 = OP("scalar", lambda e: e.activation(
                out=lora1[:], in_=p_l[0:RK, 0:TAU], func=AF.Copy, scale=2.0),
                ("ac", 1))
            evac["p_l"] = ("ac", ac_l1)

            for hc_ in range(HC):
                if hc_ % 16 == 0:
                    if hc_ == 0:
                        W_("sync", "ve", ve_xn2f)
                    else:
                        W_("sync", "pe", C["pe"])
                    bst["ld_bfc"] = DMA(
                        "sync", bfcT_sb,
                        bfcT[:, (hc_ // 16) * NSUB * D:(hc_ // 16 + 1) * NSUB * D],
                        "ldb")
                wsi = hc_ % 2
                W_("sync", "pe", bst["wq_rel"][wsi])
                wq_ld = DMA("sync", wsb[wsi][:], wfcq[hc_], "lw%d" % wsi)
                pb, pname = ((p_a, "p_a"), (p_b, "p_b"))[hc_ % 2]
                W_("tensor", "lw%d" % wsi, wq_ld)
                W_("tensor", "ldb", bst["ld_bfc"])
                W_("tensor", "ac", ac_l1)
                if hc_ == 8 and tb + 1 < NT:
                    W_("sync", "ve", ve_xn2f)
                    W_("sync", "pe", pe_aproj)
                    DMA("sync", xb3, tile_rows(tb + 1), "ldb")
                    DMA("sync", qf[:], q_st[tb + 1], "ldb")
                    bst["ld_next"] = DMA("sync", ol[:], ol_st[tb + 1], "ldb")
                wait_bank("tensor", pname)
                for dc in range(DC):
                    for ps in range(3):
                        wof = (dc * 128, dc * 128, D + dc * 128)[ps]
                        xv = (xn2h3, xn2l3, xn2h3)[ps]
                        OP("tensor", lambda e, dc=dc, pb=pb, wsi=wsi, wof=wof,
                           xv=xv, st=(dc == 0 and ps == 0): e.matmul(
                            out=pb[:, 0:TAU],
                            lhsT=wsb[wsi][:, wof:wof + 128],
                            rhs=xv[:, dc, :], start=st, stop=False))
                OP("tensor", lambda e, hc=hc_, pb=pb: e.matmul(
                    out=pb[:, 0:TAU],
                    lhsT=bfcT_sb[:, (hc % 16) * 128:(hc % 16) * 128 + 128],
                    rhs=lora1[:],
                    start=False, stop=True))
                pe_n = PEG()
                bst["wq_rel"][wsi] = pe_n
                W_("scalar", "pe", pe_n)
                ac_g = OP("scalar", lambda e, hc=hc_, pb=pb: e.activation(
                    out=hf3[:, hc, :], in_=pb[:, 0:TAU], func=AF.Gelu,
                    bias=bfc_sb[:, hc:hc + 1]), ("ac", 1))
                evac[pname] = ("ac", ac_g)
                bst["ac_gelu"][hc_] = ac_g
            ac_h = C["ac"]
            pe_b6_end = C["pe"]
            if dev and tb == 0:
                W_("gpsimd", "ac", ac_h)
                DMA("gpsimd", dbgH, h_fm, "gdbg")

            W_("tensor", "ac", ac_h)
            wait_bank("tensor", "p_l")
            for hc_ in range(HC):
                OP("tensor", lambda e, hc=hc_: e.matmul(
                    out=p_l[0:RK, 0:TAU],
                    lhsT=aprojT_sb[:, hc * RK:(hc + 1) * RK],
                    rhs=hf3[:, hc, :],
                    start=(hc == 0), stop=(hc == HC - 1)))
            pe_n = PEG()
            W_("scalar", "pe", pe_n)
            ac_l2 = OP("scalar", lambda e: e.activation(
                out=lora2[:], in_=p_l[0:RK, 0:TAU], func=AF.Copy, scale=2.0),
                ("ac", 1))
            evac["p_l"] = ("ac", ac_l2)

            wait_bank("tensor", "p_big0")
            wait_bank("tensor", "p_big1")
            ve_hsp = [0] * HC

            def emit_hsplit(hc_, pe_gate):
                W_("vector", "ac", bst["ac_gelu"][hc_])
                W_("vector", "pe", pe_gate)
                OP("vector", lambda e, hc=hc_: e.tensor_copy(
                    out=hh3[:, hc % DC, :], in_=hf3[:, hc, :]))
                ve_hsp[hc_] = OP("vector", lambda e, hc=hc_: e.tensor_sub(
                    out=hl3[:, hc % DC, :], in0=hf3[:, hc, :],
                    in1=hh3[:, hc % DC, :]), ("ve", 1))

            for hc_ in range(DC):
                emit_hsplit(hc_, pe_b6_end)
            for hc_ in range(HC):
                if tb + 1 < NT and hc_ == 24:
                    emit_ln1_stats()
                if tb + 1 < NT and hc_ in (8, 10, 12, 14, 16, 18, 20, 22):
                    g = (hc_ - 8) // 2
                    a_, ch = g // 4, g % 4
                    pq, pqn = ((p_a, "p_a"), (p_b, "p_b"))[g % 2]
                    W_("tensor", "ldb", bst["ld_next"])
                    W_("tensor", "ve", ve_S0)
                    wait_bank("tensor", pqn)
                    for d1c in range(DC):
                        OP("tensor", lambda e, d1c=d1c, a=a_, ch=ch, pq=pq:
                           e.matmul(
                            out=pq[:, 0:256],
                            lhsT=qb3[:, d1c, a * 128:a * 128 + 128],
                            rhs=S3[:, d1c, ch * 256:ch * 256 + 256],
                            start=(d1c == 0), stop=(d1c == DC - 1)))
                    pe_g = PEG()
                    W_("vector", "pe", pe_g)
                    W_("vector", "ldb", bst["ld_next"])
                    evac[pqn] = ("ve", OP("vector", lambda e, a=a_, ch=ch,
                                          pq=pq: e.tensor_add(
                        out=ofull3[:, a, ch * 256:ch * 256 + 256],
                        in0=pq[:, 0:256],
                        in1=olb3[:, a, ch * 256:ch * 256 + 256]), ("ve", 1)))
                wsi = hc_ % 2
                W_("sync", "pe", bst["wq_rel"][wsi])
                wp_ld = DMA("sync", wsb[wsi][:], wprojq[hc_], "lw%d" % wsi)
                W_("tensor", "lw%d" % wsi, wp_ld)
                W_("tensor", "ve", ve_hsp[hc_])
                for a in range(NSUB):
                    for dh in range(2):
                        for ps in range(3):
                            hv = (hh3, hh3, hl3)[ps]
                            wof = (dh * 512, D + dh * 512, dh * 512)[ps]
                            OP("tensor", lambda e, hc=hc_, a=a, hv=hv, wof=wof,
                               wsi=wsi, st=(hc_ == 0 and ps == 0): e.matmul(
                                out=p_big[a][:, (wof % D) // 512 * 512:
                                             (wof % D) // 512 * 512 + 512],
                                lhsT=hv[:, hc % DC, a * 128:a * 128 + 128],
                                rhs=wsb[wsi][:, wof:wof + 512],
                                start=st, stop=False))
                pe_n = PEG()
                bst["wq_rel"][wsi] = pe_n
                if hc_ + DC < HC:
                    emit_hsplit(hc_ + DC, pe_n)
            W_("tensor", "ac", ac_l2)
            for a in range(NSUB):
                for dh in range(2):
                    OP("tensor", lambda e, a=a, dh=dh: e.matmul(
                        out=p_big[a][:, dh * 512:dh * 512 + 512],
                        lhsT=lora2[:, a * 128:a * 128 + 128],
                        rhs=bprojT_sb[:, dh * 512:dh * 512 + 512],
                        start=False, stop=True))
            pe_y = PEG()

            for a in range(NSUB):
                W_("vector", "pe", pe_y)
                OP("vector", lambda e, a=a: e.tensor_add(
                    out=t1_3[:, a, :], in0=p_big[a][:], in1=bproj_sb[:]))
                evac["p_big%d" % a] = ("ve", OP(
                    "vector", lambda e, a=a: e.tensor_add(
                        out=fin3[:, a, :], in0=t1_3[:, a, :], in1=x2v[:, a, :]),
                    ("ve", 1)))
            ve_fin = C["ve"]

            W_("gpsimd", "ve", ve_fin)
            DMA("gpsimd", y_out[tb * TAU:(tb + 1) * TAU, :].rearrange(
                "(s p) d -> p s d", p=128), h1s3, "st")
            bst["st_done"] = C["st"]
            bst["ve_end"] = ve_fin

        if phases >= 3:
            W_("gpsimd", "st", C["st"])
            W_("gpsimd", "gst", C["gst"])
            W_("gpsimd", "gstash", C["gstash"])
            W_("gpsimd", "gcc", C["gcc"])
            if dev:
                W_("gpsimd", "gdbg", C["gdbg"])

        @block.sync
        def _(eng):
            for fn in Q["sync"]:
                fn(eng)

        @block.vector
        def _(eng):
            for fn in Q["vector"]:
                fn(eng)

        @block.scalar
        def _(eng):
            for fn in Q["scalar"]:
                fn(eng)

        @block.tensor
        def _(eng):
            for fn in Q["tensor"]:
                fn(eng)

        @block.gpsimd
        def _(eng):
            for fn in Q["gpsimd"]:
                fn(eng)

    return nc


def make_inputs(full, core):
    seq, quarter = core // 4, core % 4
    x = np.ascontiguousarray(
        np.asarray(full["L0_attn_x"])[seq, quarter * TC:(quarter + 1) * TC, :])
    d = {"x": x}
    for n in ("1q", "2q", "1k", "2k", "1v", "2v"):
        d["w_" + n] = _slab_kxo(np.asarray(full[f"L0_attn_W{n}"]))
    for n in ("1q", "1k", "1v", "2q", "2k"):
        d["b_" + n] = _fm_bias(full[f"L0_attn_b{n}"])
    d["b2v_bc"] = np.broadcast_to(full["L0_attn_b2v"], (128, D))
    d["lnw_bc"] = np.broadcast_to(full["L0_attn_ln_w"], (128, D))
    d["lnb_bc"] = np.broadcast_to(full["L0_attn_ln_b"], (128, D))
    d["bproj_bc"] = np.broadcast_to(full["L0_ffn_b_proj"], (128, D))
    d["bfc_fm"] = _fm_bias(full["L0_ffn_b_fc"])
    wfcT = np.ascontiguousarray(np.asarray(full["L0_ffn_W_fc"]).T)
    d["wfcT_r"] = _slab_kxo(wfcT)
    wprojT = np.ascontiguousarray(np.asarray(full["L0_ffn_W_proj"]).T)
    d["wprojT_r"] = wprojT.reshape(HC, 128, D)
    afcT = np.asarray(full["L0_ffn_A_fc"]).T
    d["afcT"] = afcT.reshape(DC, 128, RK).transpose(1, 0, 2).reshape(128, DC * RK)
    d["bfcT"] = np.asarray(full["L0_ffn_B_fc"]).T
    aprojT = np.asarray(full["L0_ffn_A_proj"]).T
    d["aprojT"] = aprojT.reshape(HC, 128, RK).transpose(1, 0, 2).reshape(128, HC * RK)
    d["bprojT"] = np.asarray(full["L0_ffn_B_proj"]).T
    j = np.arange(128)[:, None]
    i = np.arange(TAU)[None, :]
    m = np.concatenate([(ac * 128 + j <= i).astype(np.float32)
                        for ac in range(NSUB)], axis=1)
    d["masks"] = m
    pm = np.zeros((128, NCORE), np.float32)
    for jc in range(NCORE):
        if jc // 4 == seq and jc % 4 < quarter:
            pm[:, jc] = 1.0
    d["pmask"] = pm
    d["ident"] = np.eye(128, dtype=np.float32)
    return {k: np.ascontiguousarray(v, dtype=np.float32) for k, v in d.items()}


def kernel(**inputs):
    nc = build_nc(dev=False)
    in_maps = [make_inputs(inputs, c) for c in range(NCORE)]
    res = run_bass_kernel_spmd(nc, in_maps, core_ids=list(range(NCORE)))
    out = np.empty((B, T, D), dtype=np.float32)
    for c in range(NCORE):
        seq, quarter = c // 4, c % 4
        out[seq, quarter * TC:(quarter + 1) * TC, :] = res.results[c]["y"]
    return out


# version 29
# speedup vs baseline: 1.0229x; 1.0229x over previous
import numpy as np
from contextlib import ExitStack

import concourse.bass as bass
import concourse.mybir as mybir
from concourse.bass_utils import run_bass_kernel_spmd

F32 = mybir.dt.float32
I32 = mybir.dt.int32
AF = mybir.ActivationFunctionType
ALU = mybir.AluOpType

B, T, D, HID, RK = 2, 8192, 1024, 4096, 16
NCORE = 8
TC = T * B // NCORE
TAU = 256
NT = TC // TAU
NSUB = TAU // 128
DC = D // 128
HC = HID // 128
EPS = 1e-5
INV_SCALE = float(1.0 / np.float64(np.float32(np.sqrt(np.float64(1024 * 2048)))))
INV_D = float(1.0 / 1024.0)


def _slab_kxo(w):
    K, O = w.shape
    a = np.ascontiguousarray(w).reshape(K // 128, 128, O // 128, 128)
    return np.ascontiguousarray(a.transpose(2, 1, 0, 3).reshape(O // 128, 128, K))


def _fm_bias(b):
    return np.ascontiguousarray(np.asarray(b).reshape(-1, 128).T)


def build_nc(dev=False, phases=3, qprep=True, nt=NT, upto=99):
    nc = bass.Bass("TRN2", target_bir_lowering=False, debug=False,
                   detect_race_conditions=False)

    def din(name, shape):
        return nc.dram_tensor(name, shape, F32, kind="ExternalInput").ap()

    x_in = din("x", [TC, D])
    w_attn = {n: din("w_" + n, [DC, 128, D]) for n in ("1q", "2q", "1k", "2k", "1v", "2v")}
    bias_fm = {n: din("b_" + n, [128, DC]) for n in ("1q", "1k", "1v", "2q", "2k")}
    b2v_bc = din("b2v_bc", [128, D])
    lnw_bc = din("lnw_bc", [128, D])
    lnb_bc = din("lnb_bc", [128, D])
    bproj_bc = din("bproj_bc", [128, D])
    bfc_fm = din("bfc_fm", [128, HC])
    wfcT_r = din("wfcT_r", [HC, 128, D])
    wprojT_r = din("wprojT_r", [HC, 128, D])
    afcT = din("afcT", [128, DC * RK])
    bfcT = din("bfcT", [RK, HID])
    aprojT = din("aprojT", [128, HC * RK])
    bprojT = din("bprojT", [RK, D])
    masks = din("masks", [128, NSUB * TAU])
    pmask = din("pmask", [128, NCORE])
    ident = din("ident", [128, 128])

    y_out = nc.dram_tensor("y", [TC, D], F32, kind="ExternalOutput").ap()

    ikind = "ExternalOutput" if dev else "Internal"
    q_st = nc.dram_tensor("q_st", [NT, 128, DC * TAU], F32, kind=ikind).ap()
    ol_st = nc.dram_tensor("ol_st", [NT, 128, NSUB * D], F32, kind=ikind).ap()
    BF16 = mybir.dt.bfloat16
    wfcq = nc.dram_tensor("wfcq", [HC, 128, 2 * D], BF16, kind=ikind).ap()
    wprojq = nc.dram_tensor("wprojq", [HC, 128, 2 * D], BF16, kind=ikind).ap()
    cc_in = nc.dram_tensor("cc_in", [128, DC * D], F32, kind="Internal").ap()
    cc_out = nc.dram_tensor("cc_out", [NCORE * 128, DC * D], F32, kind="Internal", addr_space="Shared").ap()
    scal_dr = nc.dram_tensor("scal_dr", [1, 8], F32, kind="Internal").ap()
    if dev:
        dbgS = nc.dram_tensor("dbgS", [128, DC * D], F32, kind="ExternalOutput").ap()
        dbgS0 = nc.dram_tensor("dbgS0", [128, DC * D], F32, kind="ExternalOutput").ap()
        dbgH = nc.dram_tensor("dbgH", [128, HC * TAU], F32, kind="ExternalOutput").ap()
        dbgX2 = nc.dram_tensor("dbgX2", [128, NSUB * D], F32, kind="ExternalOutput").ap()

    with ExitStack() as ctx:
        sb = lambda name, shape, dt=F32: ctx.enter_context(nc.sbuf_tensor(name, shape, dt))
        ident_sb = sb("ident_sb", [128, 128])
        masks_sb = sb("masks_sb", [128, NSUB * TAU])
        pmask_sb = sb("pmask_sb", [128, NCORE])
        bias_sb = {n: sb("bias_" + n, [128, DC]) for n in bias_fm}
        b2v_sb = sb("b2v_sb", [128, D])
        lnw_sb = sb("lnw_sb", [128, D])
        lnb_sb = sb("lnb_sb", [128, D])
        bproj_sb = sb("bproj_sb", [128, D])
        bfc_sb = sb("bfc_sb", [128, HC])
        afcT_sb = sb("afcT_sb", [128, DC * RK])

        aprojT_sb = sb("aprojT_sb", [128, HC * RK])
        bprojT_sb = sb("bprojT_sb", [RK, D])
        S_sb = sb("S_sb", [128, DC * D])
        stats = sb("stats", [128, 32])
        sq_sc = sb("sq_sc", [128, D])
        qmax = sb("qmax", [128, 4])
        sbc = sb("sbc", [128, 4])
        xt = sb("xt", [128, NSUB * D])
        xn_tm = sb("xn_tm", [128, NSUB * D])
        bigbuf = sb("bigbuf", [128, HC * TAU])
        h1 = sb("h1", [128, DC * TAU])
        qf = sb("qf", [128, DC * TAU])
        kf = sb("kf", [128, DC * TAU])
        ktm = sb("ktm", [128, NSUB * D])
        vtm = sb("vtm", [128, NSUB * D])
        ol = sb("ol", [128, NSUB * D])
        ws = [sb(f"ws{i}", [128, D]) for i in range(4)]
        wsb = [sb(f"wsb{i}", [128, 2 * D], mybir.dt.bfloat16) for i in range(2)]
        qi32 = sb("qi32", [128, D], I32)
        bfa = sb("bfa", [128, 2 * DC * TAU], mybir.dt.bfloat16)
        qout = [sb(f"qout{i}", [128, 2 * D], mybir.dt.bfloat16) for i in range(2)]
        lora1 = sb("lora1", [RK, TAU])
        lora2 = sb("lora2", [RK, TAU])
        xn_fm = bigbuf[:, 0:DC * TAU]
        qraw = [bigbuf[:, DC * TAU + 1024 * i:DC * TAU + 1024 * (i + 1)]
                for i in range(2)]
        qd = bigbuf[:, 4096:5120]
        qr = bigbuf[:, 5120:6144]
        sc_sb = bigbuf[:, 6144:6144 + NSUB * TAU]
        h_fm = bigbuf[:, 0:HC * TAU]

        ps = lambda name, shape: ctx.enter_context(nc.psum_tensor(name, shape, F32))
        p_tr = ps("p_tr", [128, 256])
        p_a = ps("p_a", [128, 256])
        p_b = ps("p_b", [128, 256])
        p_big = [ps("p_big0", [128, 1024]), ps("p_big1", [128, 1024])]
        p_l = ps("p_l", [128, 256])

        SEM = {}
        for s_ in ("ldc", "ldx", "ldb", "lw0", "lw1", "lw2", "lw3",
                   "gq0", "gq1", "gst", "gstash", "gcc", "gdbg",
                   "pe", "ve", "ac", "cc", "st"):
            SEM[s_] = ctx.enter_context(nc.semaphore("sem_" + s_))
        block = ctx.enter_context(nc.Block())

        Q = {"sync": [], "vector": [], "scalar": [], "tensor": [], "gpsimd": []}
        C = {k: 0 for k in SEM}

        def W_(eng, sem, val):
            if val is None or val <= 0:
                return
            Q[eng].append(lambda e, s=sem, v=val: e.wait_ge(SEM[s], v))

        def OP(eng, fn, inc=None):
            if inc is None:
                Q[eng].append(fn)
                return None
            sem, k = inc
            C[sem] += k
            Q[eng].append(lambda e, f=fn, s=sem, kk=k: f(e).then_inc(SEM[s], kk))
            return C[sem]

        def PEG():
            C["pe"] += 1
            last = Q["tensor"].pop()
            Q["tensor"].append(lambda e, f=last: f(e).then_inc(SEM["pe"], 1))
            return C["pe"]

        def VC(fn):
            n = OP("vector", fn, ("ve", 1))
            W_("vector", "ve", n)
            return n

        def DMA(eng, out_ap, in_ap, sem):
            return OP(eng, lambda e, o=out_ap, i=in_ap: e.dma_start(out=o, in_=i),
                      (sem, 16))

        evac = {"p_a": ("ve", 0), "p_b": ("ve", 0), "p_tr": ("ve", 0),
                "p_big0": ("ve", 0), "p_big1": ("ve", 0), "p_l": ("ac", 0)}

        def wait_bank(eng, name):
            s_, v = evac[name]
            W_(eng, s_, v)

        xt3 = xt[:].rearrange("p (s d) -> p s d", s=NSUB)
        xn3 = xn_tm[:].rearrange("p (s d) -> p s d", s=NSUB)
        xnf3 = xn_fm.rearrange("p (c t) -> p c t", c=DC)
        h13 = h1[:].rearrange("p (c t) -> p c t", c=DC)
        h1s3 = h1[:].rearrange("p (s d) -> p s d", s=NSUB)
        qf3 = qf[:].rearrange("p (c t) -> p c t", c=DC)
        kf3 = kf[:].rearrange("p (c t) -> p c t", c=DC)
        kfs3 = kf[:].rearrange("p (s d) -> p s d", s=NSUB)
        ktm3 = ktm[:].rearrange("p (a d) -> p a d", a=NSUB)
        vtm3 = vtm[:].rearrange("p (a d) -> p a d", a=NSUB)
        ol3 = ol[:].rearrange("p (a d) -> p a d", a=NSUB)
        S3 = S_sb[:].rearrange("p (c d) -> p c d", c=DC)
        xn2h3 = bfa[:, 0:DC * TAU].rearrange("p (c t) -> p c t", c=DC)
        xn2l3 = bfa[:, DC * TAU:2 * DC * TAU].rearrange("p (c t) -> p c t", c=DC)
        hf3 = h_fm.rearrange("p (c t) -> p c t", c=HC)
        hh3 = xn2h3
        hl3 = xn2l3
        bfcT_sb = ktm[0:RK, 0:NSUB * D]
        msk3 = masks_sb[:].rearrange("p (a t) -> p a t", a=NSUB)

        DMA("sync", ident_sb[:], ident, "ldc")
        DMA("sync", masks_sb[:], masks, "ldc")
        DMA("sync", pmask_sb[:], pmask, "ldc")
        for n in bias_sb:
            DMA("sync", bias_sb[n][:], bias_fm[n], "ldc")
        DMA("sync", b2v_sb[:], b2v_bc, "ldc")
        DMA("sync", lnw_sb[:], lnw_bc, "ldc")
        DMA("sync", lnb_sb[:], lnb_bc, "ldc")
        DMA("sync", bproj_sb[:], bproj_bc, "ldc")
        DMA("sync", bfc_sb[:], bfc_fm, "ldc")
        DMA("sync", afcT_sb[:], afcT, "ldc")
        DMA("sync", aprojT_sb[:], aprojT, "ldc")
        DMA("sync", bprojT_sb[:], bprojT, "ldc")
        ld_consts = C["ldc"]
        OP("vector", lambda e: e.memset(S_sb[:], 0.0))
        ve_S_ready = OP("vector", lambda e: e.memset(qmax[:, 0:2], 0.0), ("ve", 1))

        def q_slab_in(i):
            return wfcT_r[i] if i < HC else wprojT_r[i - HC]

        def q_slab_out(i):
            return wfcq[i] if i < HC else wprojq[i - HC]

        qstate = {"ve_prev": 0, "gd_load": [0] * 64, "ac9": 0, "st_gd": 0,
                  "st_prev": 0, "sbc_gd": 0, "sq_guard": 0}

        def emit_qmax_slab(i):
            buf = qraw[i % 2]
            W_("gpsimd", "ve", qstate["ve_prev"])
            qstate["gd_load"][i] = DMA("gpsimd", buf, q_slab_in(i), "gq%d" % (i % 2))
            W_("vector", "gq%d" % (i % 2), qstate["gd_load"][i])
            if i == 0:
                W_("vector", "ve", ve_S_ready)
            m = 0 if i < HC else 1
            VC(lambda e, b=buf: e.reduce_max(
                out=qmax[:, 2:3], in_=b, axis=mybir.AxisListType.X,
                apply_absolute_value=True))
            qstate["ve_prev"] = VC(lambda e, m=m: e.tensor_max(
                out=qmax[:, m:m + 1], in0=qmax[:, m:m + 1], in1=qmax[:, 2:3]))

        def emit_qscalars():
            for m in range(2):
                W_("vector", "gcc", C["gcc"])
                W_("vector", "ac", C["ac"])
                VC(lambda e: e.memset(sq_sc[:, 0:128], 0.0))
                vq = VC(lambda e, m=m: e.tensor_copy(
                    out=sq_sc[:, 0:1], in_=qmax[:, m:m + 1]))
                W_("tensor", "ve", vq)
                W_("tensor", "ldc", ld_consts)
                pe_n = OP("tensor", lambda e: e.transpose(
                    out=p_tr[:, 0:128], in_=sq_sc[:, 0:128], identity=ident_sb[:]),
                    ("pe", 1))
                qstate["sq_guard"] = pe_n
                W_("vector", "pe", pe_n)
                VC(lambda e: e.reduce_max(
                    out=stats[0:1, 16:17], in_=p_tr[0:1, 0:128],
                    axis=mybir.AxisListType.X))
                VC(lambda e: e.memset(stats[0:1, 17:18], 127.0))
                VC(lambda e: e.reciprocal(
                    out=stats[0:1, 18:19], in_=stats[0:1, 17:18]))
                VC(lambda e: e.tensor_mul(
                    out=stats[0:1, 19:20], in0=stats[0:1, 16:17],
                    in1=stats[0:1, 18:19]))
                vq2 = VC(lambda e: e.reciprocal(
                    out=stats[0:1, 20:21], in_=stats[0:1, 19:20]))
                W_("gpsimd", "ve", vq2)
                DMA("gpsimd", scal_dr[0:1, 2 * m:2 * m + 1], stats[0:1, 19:20], "gcc")
                g1 = DMA("gpsimd", scal_dr[0:1, 2 * m + 1:2 * m + 2],
                         stats[0:1, 20:21], "gcc")
                W_("gpsimd", "gcc", g1)
                DMA("gpsimd", sbc[:, 2 * m:2 * m + 1],
                    scal_dr[0:1, 2 * m:2 * m + 1].broadcast_to([128, 1]), "gcc")
                qstate["sbc_gd"] = DMA(
                    "gpsimd", sbc[:, 2 * m + 1:2 * m + 2],
                    scal_dr[0:1, 2 * m + 1:2 * m + 2].broadcast_to([128, 1]), "gcc")

        def emit_qquant_slab(i):
            buf = qraw[i % 2]
            qrx = qr
            m = 0 if i < HC else 1
            W_("gpsimd", "ve", qstate["ve_prev"])
            ld_i = DMA("gpsimd", buf, q_slab_in(i), "gq%d" % (i % 2))
            W_("vector", "gq%d" % (i % 2), ld_i)
            W_("vector", "gcc", qstate["sbc_gd"])
            W_("vector", "gst", qstate["st_prev"])
            OP("vector", lambda e, b=buf, m=m: e.tensor_scalar(
                out=qd, in0=b, scalar1=sbc[:, 2 * m + 1:2 * m + 2], scalar2=None,
                op0=ALU.mult))
            OP("vector", lambda e: e.tensor_copy(out=qi32[:], in_=qd))
            OP("vector", lambda e, q=qrx: e.tensor_copy(out=q, in_=qi32[:]))
            OP("vector", lambda e, b=buf, q=qrx: e.tensor_sub(out=b, in0=qd, in1=q))
            OP("vector", lambda e, b=buf, q=qrx: e.scalar_tensor_tensor(
                out=qd, in0=b, scalar=0.5, in1=q,
                op0=ALU.is_ge, op1=ALU.add))
            OP("vector", lambda e, b=buf, q=qrx: e.tensor_scalar(
                out=q, in0=b, scalar1=-0.5, scalar2=None, op0=ALU.is_lt))
            OP("vector", lambda e, b=buf, q=qrx: e.tensor_sub(out=b, in0=qd, in1=q))
            OP("vector", lambda e, b=buf: e.tensor_scalar(
                out=qd, in0=b, scalar1=-128.0, scalar2=127.0,
                op0=ALU.max, op1=ALU.min))
            qo = qout[i % 2]
            OP("vector", lambda e, m=m: e.tensor_scalar(
                out=qr, in0=qd, scalar1=sbc[:, 2 * m:2 * m + 1], scalar2=None,
                op0=ALU.mult))
            OP("vector", lambda e, q=qo: e.tensor_copy(out=q[:, 0:D], in_=qr))
            qstate["ve_prev"] = OP("vector", lambda e, q=qo: e.tensor_sub(
                out=q[:, D:2 * D], in0=qr, in1=q[:, 0:D]), ("ve", 1))
            W_("gpsimd", "ve", qstate["ve_prev"])
            qstate["st_prev"] = qstate["st_gd"]
            qstate["st_gd"] = DMA("gpsimd", q_slab_out(i), qo[:], "gst")

        qsched = {t: [] for t in range(NT)}
        for i in range(64 if qprep else 0):
            qsched[i // 32].append(("max", i))
        quant_per_tile = [11, 11, 11, 11, 10, 10] if qprep else []
        qi = 0
        for ti, n_ in enumerate(quant_per_tile):
            for _ in range(n_):
                qsched[2 + ti].append(("quant", qi))
                qi += 1
        assert qi == 64 or not qprep

        wnames = ["1q", "2q", "1k", "2k", "1v", "2v"]
        ws_release = [0, 0, 0, 0]
        ws_loaded = {}
        prev = {"ve_ln": 0, "ve_S": ve_S_ready, "gd_stash": 0}

        def tile_rows(t):
            return x_in[t * TAU:(t + 1) * TAU, :].rearrange("(s p) d -> p s d", p=128)

        ph1 = {}
        ph_ln = {}

        def emit_ln(t):
            W_("sync", "ve", prev["ve_ln"])
            ld_x = DMA("sync", xt3, tile_rows(t), "ldx")
            ph_ln[(t, "ldx")] = ld_x
            W_("vector", "gstash", prev["gd_stash"])
            W_("scalar", "gstash", prev["gd_stash"])
            W_("scalar", "pe", qstate["sq_guard"])
            for s_ in range(NSUB):
                W_("vector", "ldx", ld_x)
                VC(lambda e, s=s_: e.reduce_sum(
                    out=stats[:, 4 * s:4 * s + 1], in_=xt3[:, s, :],
                    axis=mybir.AxisListType.X))
                W_("scalar", "ldx", ld_x)
                ac_sq = OP("scalar", lambda e, s=s_: e.activation(
                    out=sq_sc[:], in_=xt3[:, s, :], func=AF.Square,
                    accum_out=stats[:, 4 * s + 1:4 * s + 2]), ("ac", 1))
                W_("vector", "ac", ac_sq)
                VC(lambda e, s=s_: e.tensor_scalar(
                    out=stats[:, 4 * s:4 * s + 1], in0=stats[:, 4 * s:4 * s + 1],
                    scalar1=INV_D, scalar2=None, op0=ALU.mult))
                VC(lambda e, s=s_: e.tensor_scalar(
                    out=stats[:, 4 * s + 1:4 * s + 2],
                    in0=stats[:, 4 * s + 1:4 * s + 2],
                    scalar1=INV_D, scalar2=None, op0=ALU.mult))
                VC(lambda e, s=s_: e.tensor_mul(
                    out=stats[:, 4 * s + 2:4 * s + 3], in0=stats[:, 4 * s:4 * s + 1],
                    in1=stats[:, 4 * s:4 * s + 1]))
                vv = VC(lambda e, s=s_: e.scalar_tensor_tensor(
                    out=stats[:, 4 * s + 2:4 * s + 3],
                    in0=stats[:, 4 * s + 1:4 * s + 2], scalar=EPS,
                    in1=stats[:, 4 * s + 2:4 * s + 3],
                    op0=ALU.add, op1=ALU.subtract))
                W_("scalar", "ve", vv)
                ac_sd = OP("scalar", lambda e, s=s_: e.activation(
                    out=stats[:, 4 * s + 3:4 * s + 4],
                    in_=stats[:, 4 * s + 2:4 * s + 3], func=AF.Sqrt), ("ac", 1))
                W_("vector", "ac", ac_sd)
                VC(lambda e, s=s_: e.reciprocal(
                    out=stats[:, 4 * s + 2:4 * s + 3],
                    in_=stats[:, 4 * s + 3:4 * s + 4]))
                OP("vector", lambda e, s=s_: e.tensor_scalar(
                    out=xn3[:, s, :], in0=xt3[:, s, :],
                    scalar1=stats[:, 4 * s:4 * s + 1],
                    scalar2=stats[:, 4 * s + 2:4 * s + 3],
                    op0=ALU.subtract, op1=ALU.mult), ("ve", 1))
            prev["ve_ln"] = C["ve"]
            ph_ln[t] = C["ve"]

        def emit_tr(t):
            ve_xn = ph_ln[t]
            ld_x = ph_ln[(t, "ldx")]
            for dc in range(DC):
                pb, pname = ((p_a, "p_a"), (p_b, "p_b"))[dc % 2]
                W_("tensor", "ve", ve_xn)
                wait_bank("tensor", pname)
                W_("tensor", "ldc", ld_consts)
                for s_ in range(NSUB):
                    OP("tensor", lambda e, s=s_, dc=dc, pb=pb: e.transpose(
                        out=pb[:, s * 128:s * 128 + 128],
                        in_=xn3[:, s, dc * 128:dc * 128 + 128],
                        identity=ident_sb[:]))
                pe_n = PEG()
                W_("vector", "pe", pe_n)
                evac[pname] = ("ve", OP("vector", lambda e, dc=dc, pb=pb: e.tensor_copy(
                    out=xnf3[:, dc, :], in_=pb[:, 0:TAU]), ("ve", 1)))
            ph1[t] = (ld_x, C["ve"])

        def emit_phase2(t, mid_hook=None):
            ld_x, ve_xnf = ph1[t]
            wsi_cnt = {"i": 0}

            def next_wsi():
                v = wsi_cnt["i"] % 4
                wsi_cnt["i"] += 1
                return v

            def load_wslab(mat, slab, wsi):
                W_("sync", "pe", ws_release[wsi])
                ws_loaded[(t, mat, slab)] = (
                    "lw%d" % wsi,
                    DMA("sync", ws[wsi][:], w_attn[mat][slab], "lw%d" % wsi))

            def mlp_h1(mat, bias_name):
                for hc_ in range(DC):
                    wsi = next_wsi()
                    load_wslab(mat, hc_, wsi)
                    pb, pname = ((p_a, "p_a"), (p_b, "p_b"))[hc_ % 2]
                    W_("tensor", *ws_loaded[(t, mat, hc_)])
                    W_("tensor", "ve", ve_xnf)
                    wait_bank("tensor", pname)
                    for kc in range(DC):
                        OP("tensor", lambda e, kc=kc, pb=pb, wsi=wsi: e.matmul(
                            out=pb[:, 0:TAU],
                            lhsT=ws[wsi][:, kc * 128:kc * 128 + 128],
                            rhs=xnf3[:, kc, :],
                            start=(kc == 0), stop=(kc == DC - 1)))
                    pe_n = PEG()
                    ws_release[wsi] = pe_n
                    W_("scalar", "pe", pe_n)
                    evac[pname] = ("ac", OP("scalar", lambda e, hc=hc_, pb=pb,
                                            bn=bias_name: e.activation(
                        out=h13[:, hc, :], in_=pb[:, 0:TAU], func=AF.Gelu,
                        bias=bias_sb[bn][:, hc:hc + 1]), ("ac", 1)))
                return C["ac"]

            def mlp_out_fm(mat, bias_name, dst3, ac_h1):
                last = 0
                for oc in range(DC):
                    wsi = next_wsi()
                    load_wslab(mat, oc, wsi)
                    pb, pname = ((p_a, "p_a"), (p_b, "p_b"))[oc % 2]
                    W_("tensor", *ws_loaded[(t, mat, oc)])
                    W_("tensor", "ac", ac_h1)
                    wait_bank("tensor", pname)
                    for hc_ in range(DC):
                        OP("tensor", lambda e, hc=hc_, pb=pb, wsi=wsi: e.matmul(
                            out=pb[:, 0:TAU],
                            lhsT=ws[wsi][:, hc * 128:hc * 128 + 128],
                            rhs=h13[:, hc, :],
                            start=(hc == 0), stop=(hc == DC - 1)))
                    pe_n = PEG()
                    ws_release[wsi] = pe_n
                    W_("scalar", "pe", pe_n)
                    last = OP("scalar", lambda e, oc=oc, pb=pb, bn=bias_name,
                              d3=dst3: e.activation(
                        out=d3[:, oc, :], in_=pb[:, 0:TAU], func=AF.Identity,
                        bias=bias_sb[bn][:, oc:oc + 1]), ("ac", 1))
                    evac[pname] = ("ac", last)
                return last

            def mlp_out_vtm(ac_h1):
                wait_bank("tensor", "p_big0")
                wait_bank("tensor", "p_big1")
                W_("tensor", "ac", ac_h1)
                pe_n = 0
                for oc in range(DC):
                    wsi = next_wsi()
                    load_wslab("2v", oc, wsi)
                    W_("tensor", *ws_loaded[(t, "2v", oc)])
                    for a in range(NSUB):
                        for hc_ in range(DC):
                            OP("tensor", lambda e, hc=hc_, oc=oc, a=a, wsi=wsi:
                               e.matmul(
                                out=p_big[a][:, oc * 128:oc * 128 + 128],
                                lhsT=h13[:, hc, a * 128:a * 128 + 128],
                                rhs=ws[wsi][:, hc * 128:hc * 128 + 128],
                                start=(hc == 0), stop=(hc == DC - 1)))
                    pe_n = PEG()
                    ws_release[wsi] = pe_n
                for a in range(NSUB):
                    W_("vector", "pe", pe_n)
                    evac["p_big%d" % a] = ("ve", OP(
                        "vector", lambda e, a=a: e.tensor_add(
                            out=vtm3[:, a, :], in0=p_big[a][:], in1=b2v_sb[:]),
                        ("ve", 1)))

            if upto < 3:
                return
            ac_h1q = mlp_h1("1q", "1q")
            if mid_hook is not None:
                mid_hook("a")
            qf_done = mlp_out_fm("2q", "2q", qf3, ac_h1q)
            if mid_hook is not None:
                mid_hook("b")
            if upto < 4:
                return
            ac_h1k = mlp_h1("1k", "1k")
            if mid_hook is not None:
                mid_hook("c")
            kf_done = mlp_out_fm("2k", "2k", kf3, ac_h1k)
            if mid_hook is not None:
                mid_hook("d")
            if upto < 5:
                return
            ac_h1v = mlp_h1("1v", "1v")
            if mid_hook is not None:
                mid_hook("e")
            mlp_out_vtm(ac_h1v)
            ve_vtm = C["ve"]

            if upto < 6:
                return
            for a in range(NSUB):
                pbn = "p_big%d" % (a % 2)
                W_("tensor", "ac", kf_done)
                wait_bank("tensor", pbn)
                for dc in range(DC):
                    OP("tensor", lambda e, dc=dc, a=a: e.transpose(
                        out=p_big[a % 2][:, dc * 128:dc * 128 + 128],
                        in_=kf3[:, dc, a * 128:a * 128 + 128],
                        identity=ident_sb[:]))
                pe_n = PEG()
                W_("vector", "pe", pe_n)
                evac[pbn] = ("ve", OP("vector", lambda e, a=a: e.tensor_copy(
                    out=ktm3[:, a, :], in_=p_big[a % 2][:]), ("ve", 1)))
            ve_ktm = C["ve"]

            if upto < 7:
                return
            for ac_ in range(NSUB):
                pb, pname = ((p_a, "p_a"), (p_b, "p_b"))[ac_ % 2]
                W_("tensor", "ac", max(qf_done, kf_done))
                wait_bank("tensor", pname)
                for kc in range(DC):
                    OP("tensor", lambda e, kc=kc, ac=ac_, pb=pb: e.matmul(
                        out=pb[:, 0:TAU],
                        lhsT=kf3[:, kc, ac * 128:ac * 128 + 128],
                        rhs=qf3[:, kc, :],
                        start=(kc == 0), stop=(kc == DC - 1)))
                pe_n = PEG()
                W_("vector", "pe", pe_n)
                evac[pname] = ("ve", OP("vector", lambda e, ac=ac_, pb=pb:
                                        e.scalar_tensor_tensor(
                    out=sc_sb[:, ac * TAU:(ac + 1) * TAU], in0=pb[:, 0:TAU],
                    scalar=INV_SCALE, in1=msk3[:, ac, :],
                    op0=ALU.mult, op1=ALU.mult), ("ve", 1)))
            ve_sT = C["ve"]

            if upto < 8:
                return
            for a in range(NSUB):
                pbn = "p_big%d" % a
                W_("tensor", "ve", max(ve_sT, ve_vtm, prev["ve_S"]))
                wait_bank("tensor", pbn)
                for d1c in range(DC):
                    for dh in range(2):
                        OP("tensor", lambda e, d1c=d1c, dh=dh, a=a: e.matmul(
                            out=p_big[a][:, dh * 512:dh * 512 + 512],
                            lhsT=qf3[:, d1c, a * 128:a * 128 + 128],
                            rhs=S3[:, d1c, dh * 512:dh * 512 + 512],
                            start=(d1c == 0), stop=False))
                for ac_ in range(NSUB):
                    for dh in range(2):
                        OP("tensor", lambda e, ac=ac_, dh=dh, a=a: e.matmul(
                            out=p_big[a][:, dh * 512:dh * 512 + 512],
                            lhsT=sc_sb[:, ac * TAU + a * 128:ac * TAU + a * 128 + 128],
                            rhs=vtm3[:, ac, dh * 512:dh * 512 + 512],
                            start=False, stop=(ac == NSUB - 1)))
                pe_n = PEG()
                W_("vector", "pe", pe_n)
                evac[pbn] = ("ve", OP("vector", lambda e, a=a: e.tensor_copy(
                    out=ol3[:, a, :], in_=p_big[a][:]), ("ve", 1)))
            ve_ol = C["ve"]
            pe_qS_done = C["pe"]

            if upto < 9:
                return
            for d1c in range(DC):
                pbn = "p_big%d" % (d1c % 2)
                wait_bank("tensor", pbn)
                W_("tensor", "ve", ve_ktm)
                for a in range(NSUB):
                    for dh in range(2):
                        OP("tensor", lambda e, a=a, dh=dh, d1c=d1c: e.matmul(
                            out=p_big[d1c % 2][:, dh * 512:dh * 512 + 512],
                            lhsT=ktm3[:, a, d1c * 128:d1c * 128 + 128],
                            rhs=vtm3[:, a, dh * 512:dh * 512 + 512],
                            start=(a == 0), stop=(a == NSUB - 1)))
                pe_n = PEG()
                W_("vector", "pe", pe_n)
                evac[pbn] = ("ve", OP("vector", lambda e, d1c=d1c: e.scalar_tensor_tensor(
                    out=S3[:, d1c, :], in0=p_big[d1c % 2][:], scalar=INV_SCALE,
                    in1=S3[:, d1c, :], op0=ALU.mult, op1=ALU.add), ("ve", 1)))
            prev["ve_S"] = C["ve"]

            if upto < 10:
                return
            W_("gpsimd", "gstash", prev["gd_stash"])
            W_("gpsimd", "ve", ve_ol)
            DMA("gpsimd", ol_st[t], ol[:], "gstash")
            W_("gpsimd", "ac", qf_done)
            W_("gpsimd", "pe", pe_qS_done)
            DMA("gpsimd", q_st[t], qf[:], "gstash")
            prev["gd_stash"] = C["gstash"]

            return

        def hook_fn(t, stage):
            if stage == "b":
                if t + 1 < nt:
                    emit_ln(t + 1)
                return
            items = qsched.get(t, [])
            k4 = (len(items) + 3) // 4 if items else 0
            idx = {"a": 0, "c": 1, "d": 2, "e": 3}[stage]
            for kind, i in items[idx * k4:(idx + 1) * k4]:
                if kind == "max":
                    emit_qmax_slab(i)
                else:
                    emit_qquant_slab(i)

        emit_ln(0)
        emit_tr(0)
        for t in range(nt):
            emit_phase2(t, lambda stage, tt=t: hook_fn(tt, stage))
            if t + 1 < nt:
                emit_tr(t + 1)
            if t == 1 and qprep:
                emit_qscalars()

        pe_sweepA = C["pe"]

        if phases == 1:
            if dev:
                W_("gpsimd", "ve", prev["ve_S"])
                DMA("gpsimd", dbgS, S_sb[:], "gdbg")
                W_("gpsimd", "gdbg", C["gdbg"])
            W_("gpsimd", "gstash", C["gstash"])
            W_("gpsimd", "gst", C["gst"])
        phase_c_on = phases >= 2
        ve_S0 = 0
        if phase_c_on:
          W_("gpsimd", "ve", prev["ve_S"])
          gcc_ccin = DMA("gpsimd", cc_in, S_sb[:], "gcc")
          W_("gpsimd", "gcc", C["gcc"])
          OP("gpsimd", lambda e: e.collective_compute(
              "AllGather", ALU.bypass,
              replica_groups=[[0, 1, 2, 3, 4, 5, 6, 7]],
              ins=[cc_in], outs=[cc_out]), ("cc", 1))
          W_("gpsimd", "cc", C["cc"])
          if dev:
              DMA("gpsimd", dbgS, S_sb[:], "gdbg")
          W_("gpsimd", "pe", pe_sweepA)
          W_("gpsimd", "gstash", C["gstash"])
          W_("gpsimd", "gst", C["gst"])
          W_("vector", "gcc", gcc_ccin)
          OP("vector", lambda e: e.memset(S_sb[:], 0.0), ("ve", 1))
          W_("sync", "pe", pe_sweepA)
          W_("sync", "gst", C["gst"])
          W_("sync", "gstash", C["gstash"])
          W_("sync", "cc", C["cc"])
          HB = DC * D // 2
          ve_half = {0: C["ve"], 1: C["ve"]}
          for j in range(NCORE):
              for h in range(2):
                  qn, sn = (("gpsimd", "gcc"), ("sync", "ldx"))[h]
                  W_(qn, "ve", ve_half[h])
                  gd_j = DMA(qn, bigbuf[:, h * HB:(h + 1) * HB],
                             cc_out[j * 128:(j + 1) * 128, h * HB:(h + 1) * HB],
                             sn)
                  W_("vector", sn, gd_j)
                  ve_half[h] = OP("vector", lambda e, j=j, h=h: e.scalar_tensor_tensor(
                      out=S_sb[:, h * HB:(h + 1) * HB],
                      in0=bigbuf[:, h * HB:(h + 1) * HB],
                      scalar=pmask_sb[:, j:j + 1],
                      in1=S_sb[:, h * HB:(h + 1) * HB],
                      op0=ALU.mult, op1=ALU.add), ("ve", 1))
          ve_S0 = C["ve"]
          if dev:
              W_("gpsimd", "ve", ve_S0)
              DMA("gpsimd", dbgS0, S_sb[:], "gdbg")
          if phases == 2:
              W_("gpsimd", "gcc", C["gcc"])
              W_("gpsimd", "gst", C["gst"])
              W_("gpsimd", "gstash", C["gstash"])
              if dev:
                  W_("gpsimd", "gdbg", C["gdbg"])

        xb3, qb3, olb3, ofull3 = xt3, qf3, ol3, xn3
        x2v = kfs3
        xn2tm3 = vtm3
        xn2fm3 = qf3
        hat3 = h1s3
        t1_3 = ktm3
        fin3 = h1s3
        def stp(base, k):
            return stats[:, base:base + 8].rearrange(
                "p (s c) -> p s c", c=4)[:, :, k:k + 1]

        def emit_ln1_stats():
            o = 8
            for s_ in range(NSUB):
                VC(lambda e, s=s_, o=o: e.reduce_sum(
                    out=stats[:, o + 4 * s:o + 4 * s + 1], in_=ofull3[:, s, :],
                    axis=mybir.AxisListType.X))
            W_("scalar", "ve", C["ve"])
            ac_sq = 0
            for s_ in range(NSUB):
                ac_sq = OP("scalar", lambda e, s=s_, o=o: e.activation(
                    out=sq_sc[:], in_=ofull3[:, s, :], func=AF.Square,
                    accum_out=stats[:, o + 4 * s + 1:o + 4 * s + 2]), ("ac", 1))
            W_("vector", "ac", ac_sq)
            VC(lambda e, o=o: e.tensor_scalar(
                out=stp(o, 0), in0=stp(o, 0),
                scalar1=INV_D, scalar2=None, op0=ALU.mult))
            VC(lambda e, o=o: e.tensor_scalar(
                out=stp(o, 1), in0=stp(o, 1),
                scalar1=INV_D, scalar2=None, op0=ALU.mult))
            VC(lambda e, o=o: e.tensor_mul(
                out=stp(o, 2), in0=stp(o, 0), in1=stp(o, 0)))
            vv = VC(lambda e, o=o: e.scalar_tensor_tensor(
                out=stp(o, 2), in0=stp(o, 1), scalar=EPS, in1=stp(o, 2),
                op0=ALU.add, op1=ALU.subtract))
            W_("scalar", "ve", vv)
            ac_sd = OP("scalar", lambda e, o=o: e.activation(
                out=stp(o, 3), in_=stp(o, 2), func=AF.Sqrt), ("ac", 1))
            W_("vector", "ac", ac_sd)
            VC(lambda e, o=o: e.reciprocal(out=stp(o, 2), in_=stp(o, 3)))

        bst = {"st_done": 0, "ve_end": ve_S0,
               "wq_rel": [pe_sweepA, pe_sweepA], "ac_gelu": [0] * HC,
               "lora_free": ("ac", 0)}

        for tb in range(NT if phases >= 3 else 0):
            W_("sync", "ve", bst["ve_end"])
            W_("sync", "st", bst["st_done"])
            if tb == 0:
                W_("sync", "gstash", C["gstash"])
                W_("sync", "gst", C["gst"])
                W_("sync", "cc", C["cc"])
                DMA("sync", xb3, tile_rows(tb), "ldb")
                DMA("sync", qf[:], q_st[tb], "ldb")
                ld_b = DMA("sync", ol[:], ol_st[tb], "ldb")
            else:
                ld_b = bst["ld_next"]

            if tb == 0:
                for a in range(NSUB):
                    pbn = "p_big%d" % a
                    W_("tensor", "ldb", ld_b)
                    W_("tensor", "ve", ve_S0)
                    wait_bank("tensor", pbn)
                    for d1c in range(DC):
                        for dh in range(2):
                            OP("tensor", lambda e, d1c=d1c, dh=dh, a=a: e.matmul(
                                out=p_big[a][:, dh * 512:dh * 512 + 512],
                                lhsT=qb3[:, d1c, a * 128:a * 128 + 128],
                                rhs=S3[:, d1c, dh * 512:dh * 512 + 512],
                                start=(d1c == 0), stop=(d1c == DC - 1)))
                    pe_n = PEG()
                    W_("vector", "pe", pe_n)
                    W_("vector", "ldb", ld_b)
                    evac[pbn] = ("ve", OP("vector", lambda e, a=a: e.tensor_add(
                        out=ofull3[:, a, :], in0=p_big[a][:], in1=olb3[:, a, :]),
                        ("ve", 1)))

            if tb == 0:
                emit_ln1_stats()
                o = 8
                for s_ in range(NSUB):
                    OP("vector", lambda e, s=s_, o=o: e.tensor_scalar(
                        out=xn2tm3[:, s, :], in0=ofull3[:, s, :],
                        scalar1=stats[:, o + 4 * s:o + 4 * s + 1],
                        scalar2=stats[:, o + 4 * s + 2:o + 4 * s + 3],
                        op0=ALU.subtract, op1=ALU.mult))
                    OP("vector", lambda e, s=s_: e.tensor_mul(
                        out=xn2tm3[:, s, :], in0=xn2tm3[:, s, :],
                        in1=lnw_sb[:, :]))
                    OP("vector", lambda e, s=s_: e.tensor_add(
                        out=xn2tm3[:, s, :], in0=xn2tm3[:, s, :],
                        in1=lnb_sb[:, :]))
            for s_ in range(NSUB):
                W_("vector", "ldb", ld_b)
                OP("vector", lambda e, s=s_: e.tensor_add(
                    out=x2v[:, s, :], in0=xn2tm3[:, s, :], in1=xb3[:, s, :]),
                    ("ve", 1))
            o = 16
            for s_ in range(NSUB):
                VC(lambda e, s=s_, o=o: e.reduce_sum(
                    out=stats[:, o + 4 * s:o + 4 * s + 1], in_=x2v[:, s, :],
                    axis=mybir.AxisListType.X))
            W_("scalar", "ve", C["ve"])
            ac_sq = 0
            for s_ in range(NSUB):
                ac_sq = OP("scalar", lambda e, s=s_, o=o: e.activation(
                    out=sq_sc[:], in_=x2v[:, s, :], func=AF.Square,
                    accum_out=stats[:, o + 4 * s + 1:o + 4 * s + 2]), ("ac", 1))
            W_("vector", "ac", ac_sq)
            VC(lambda e, o=o: e.tensor_scalar(
                out=stp(o, 0), in0=stp(o, 0),
                scalar1=INV_D, scalar2=None, op0=ALU.mult))
            VC(lambda e, o=o: e.tensor_scalar(
                out=stp(o, 1), in0=stp(o, 1),
                scalar1=INV_D, scalar2=None, op0=ALU.mult))
            VC(lambda e, o=o: e.tensor_mul(
                out=stp(o, 2), in0=stp(o, 0), in1=stp(o, 0)))
            vv = VC(lambda e, o=o: e.scalar_tensor_tensor(
                out=stp(o, 2), in0=stp(o, 1), scalar=EPS, in1=stp(o, 2),
                op0=ALU.add, op1=ALU.subtract))
            W_("scalar", "ve", vv)
            ac_sd = OP("scalar", lambda e, o=o: e.activation(
                out=stp(o, 3), in_=stp(o, 2), func=AF.Sqrt), ("ac", 1))
            W_("vector", "ac", ac_sd)
            VC(lambda e, o=o: e.reciprocal(out=stp(o, 2), in_=stp(o, 3)))
            for s_ in range(NSUB):
                OP("vector", lambda e, s=s_, o=o: e.tensor_scalar(
                    out=xn2tm3[:, s, :], in0=x2v[:, s, :],
                    scalar1=stats[:, o + 4 * s:o + 4 * s + 1],
                    scalar2=stats[:, o + 4 * s + 2:o + 4 * s + 3],
                    op0=ALU.subtract, op1=ALU.mult), ("ve", 1))
            ve_xn2 = C["ve"]
            if dev and tb == 0:
                W_("gpsimd", "ve", ve_xn2)
                DMA("gpsimd", dbgX2, kf[:], "gdbg")

            for dc in range(DC):
                pb, pname = ((p_a, "p_a"), (p_b, "p_b"))[dc % 2]
                W_("tensor", "ve", ve_xn2)
                wait_bank("tensor", pname)
                for s_ in range(NSUB):
                    OP("tensor", lambda e, s=s_, dc=dc, pb=pb: e.transpose(
                        out=pb[:, s * 128:s * 128 + 128],
                        in_=xn2tm3[:, s, dc * 128:dc * 128 + 128],
                        identity=ident_sb[:]))
                pe_n = PEG()
                W_("scalar", "pe", pe_n)
                ac_hi = OP("scalar", lambda e, dc=dc, pb=pb: e.activation(
                    out=xn2h3[:, dc, :], in_=pb[:, 0:TAU], func=AF.Copy), ("ac", 1))
                W_("vector", "pe", pe_n)
                W_("vector", "ac", ac_hi)
                OP("vector", lambda e, dc=dc, pb=pb: e.tensor_copy(
                    out=xn2fm3[:, dc, :], in_=pb[:, 0:TAU]))
                evac[pname] = ("ve", OP("vector", lambda e, dc=dc, pb=pb:
                                        e.tensor_sub(
                    out=xn2l3[:, dc, :], in0=pb[:, 0:TAU],
                    in1=xn2h3[:, dc, :]), ("ve", 1)))
            ve_xn2f = C["ve"]

            bst["pe_b6start"] = 0
            W_("tensor", "ve", ve_xn2f)
            W_("tensor", "ldc", ld_consts)
            wait_bank("tensor", "p_l")
            for dc in range(DC):
                OP("tensor", lambda e, dc=dc: e.matmul(
                    out=p_l[0:RK, 0:TAU],
                    lhsT=afcT_sb[:, dc * RK:(dc + 1) * RK],
                    rhs=xn2fm3[:, dc, :],
                    start=(dc == 0), stop=(dc == DC - 1)))
            pe_aproj = PEG()
            W_("scalar", "pe", pe_aproj)
            ac_l1 = OP("scalar", lambda e: e.activation(
                out=lora1[:], in_=p_l[0:RK, 0:TAU], func=AF.Copy, scale=2.0),
                ("ac", 1))
            evac["p_l"] = ("ac", ac_l1)

            for hc_ in range(HC):
                if hc_ % 16 == 0:
                    if hc_ == 0:
                        W_("sync", "ve", ve_xn2f)
                    else:
                        W_("sync", "pe", C["pe"])
                    bst["ld_bfc"] = DMA(
                        "sync", bfcT_sb,
                        bfcT[:, (hc_ // 16) * NSUB * D:(hc_ // 16 + 1) * NSUB * D],
                        "ldb")
                wsi = hc_ % 2
                W_("sync", "pe", bst["wq_rel"][wsi])
                wq_ld = DMA("sync", wsb[wsi][:], wfcq[hc_], "lw%d" % wsi)
                pb, pname = ((p_a, "p_a"), (p_b, "p_b"))[hc_ % 2]
                W_("tensor", "lw%d" % wsi, wq_ld)
                W_("tensor", "ldb", bst["ld_bfc"])
                W_("tensor", "ac", ac_l1)
                if hc_ == 8 and tb + 1 < NT:
                    W_("sync", "ve", ve_xn2f)
                    W_("sync", "pe", pe_aproj)
                    DMA("sync", xb3, tile_rows(tb + 1), "ldb")
                    DMA("sync", qf[:], q_st[tb + 1], "ldb")
                    bst["ld_next"] = DMA("sync", ol[:], ol_st[tb + 1], "ldb")
                wait_bank("tensor", pname)
                for dc in range(DC):
                    for ps in range(3):
                        wof = (dc * 128, dc * 128, D + dc * 128)[ps]
                        xv = (xn2h3, xn2l3, xn2h3)[ps]
                        OP("tensor", lambda e, dc=dc, pb=pb, wsi=wsi, wof=wof,
                           xv=xv, st=(dc == 0 and ps == 0): e.matmul(
                            out=pb[:, 0:TAU],
                            lhsT=wsb[wsi][:, wof:wof + 128],
                            rhs=xv[:, dc, :], start=st, stop=False))
                OP("tensor", lambda e, hc=hc_, pb=pb: e.matmul(
                    out=pb[:, 0:TAU],
                    lhsT=bfcT_sb[:, (hc % 16) * 128:(hc % 16) * 128 + 128],
                    rhs=lora1[:],
                    start=False, stop=True))
                pe_n = PEG()
                bst["wq_rel"][wsi] = pe_n
                W_("scalar", "pe", pe_n)
                ac_g = OP("scalar", lambda e, hc=hc_, pb=pb: e.activation(
                    out=hf3[:, hc, :], in_=pb[:, 0:TAU], func=AF.Gelu,
                    bias=bfc_sb[:, hc:hc + 1]), ("ac", 1))
                evac[pname] = ("ac", ac_g)
                bst["ac_gelu"][hc_] = ac_g
            ac_h = C["ac"]
            pe_b6_end = C["pe"]
            if dev and tb == 0:
                W_("gpsimd", "ac", ac_h)
                DMA("gpsimd", dbgH, h_fm, "gdbg")

            W_("tensor", "ac", ac_h)
            wait_bank("tensor", "p_l")
            for hc_ in range(HC):
                OP("tensor", lambda e, hc=hc_: e.matmul(
                    out=p_l[0:RK, 0:TAU],
                    lhsT=aprojT_sb[:, hc * RK:(hc + 1) * RK],
                    rhs=hf3[:, hc, :],
                    start=(hc == 0), stop=(hc == HC - 1)))
            pe_n = PEG()
            W_("scalar", "pe", pe_n)
            ac_l2 = OP("scalar", lambda e: e.activation(
                out=lora2[:], in_=p_l[0:RK, 0:TAU], func=AF.Copy, scale=2.0),
                ("ac", 1))
            evac["p_l"] = ("ac", ac_l2)

            wait_bank("tensor", "p_big0")
            wait_bank("tensor", "p_big1")
            ve_hsp = [0] * HC

            def emit_hsplit(hc_, pe_gate):
                W_("vector", "ac", bst["ac_gelu"][hc_])
                W_("vector", "pe", pe_gate)
                OP("vector", lambda e, hc=hc_: e.tensor_copy(
                    out=hh3[:, hc % DC, :], in_=hf3[:, hc, :]))
                ve_hsp[hc_] = OP("vector", lambda e, hc=hc_: e.tensor_sub(
                    out=hl3[:, hc % DC, :], in0=hf3[:, hc, :],
                    in1=hh3[:, hc % DC, :]), ("ve", 1))

            for hc_ in range(DC):
                emit_hsplit(hc_, pe_b6_end)
            for hc_ in range(HC):
                if tb + 1 < NT and hc_ == 24:
                    emit_ln1_stats()
                    for s_ in range(NSUB):
                        OP("vector", lambda e, s=s_: e.tensor_scalar(
                            out=xn2tm3[:, s, :], in0=ofull3[:, s, :],
                            scalar1=stats[:, 8 + 4 * s:8 + 4 * s + 1],
                            scalar2=stats[:, 8 + 4 * s + 2:8 + 4 * s + 3],
                            op0=ALU.subtract, op1=ALU.mult))
                        OP("vector", lambda e, s=s_: e.tensor_mul(
                            out=xn2tm3[:, s, :], in0=xn2tm3[:, s, :],
                            in1=lnw_sb[:, :]))
                        OP("vector", lambda e, s=s_: e.tensor_add(
                            out=xn2tm3[:, s, :], in0=xn2tm3[:, s, :],
                            in1=lnb_sb[:, :]))
                if tb + 1 < NT and hc_ in (8, 10, 12, 14, 16, 18, 20, 22):
                    g = (hc_ - 8) // 2
                    a_, ch = g // 4, g % 4
                    pq, pqn = ((p_a, "p_a"), (p_b, "p_b"))[g % 2]
                    W_("tensor", "ldb", bst["ld_next"])
                    W_("tensor", "ve", ve_S0)
                    wait_bank("tensor", pqn)
                    for d1c in range(DC):
                        OP("tensor", lambda e, d1c=d1c, a=a_, ch=ch, pq=pq:
                           e.matmul(
                            out=pq[:, 0:256],
                            lhsT=qb3[:, d1c, a * 128:a * 128 + 128],
                            rhs=S3[:, d1c, ch * 256:ch * 256 + 256],
                            start=(d1c == 0), stop=(d1c == DC - 1)))
                    pe_g = PEG()
                    W_("vector", "pe", pe_g)
                    W_("vector", "ldb", bst["ld_next"])
                    evac[pqn] = ("ve", OP("vector", lambda e, a=a_, ch=ch,
                                          pq=pq: e.tensor_add(
                        out=ofull3[:, a, ch * 256:ch * 256 + 256],
                        in0=pq[:, 0:256],
                        in1=olb3[:, a, ch * 256:ch * 256 + 256]), ("ve", 1)))
                wsi = hc_ % 2
                W_("sync", "pe", bst["wq_rel"][wsi])
                wp_ld = DMA("sync", wsb[wsi][:], wprojq[hc_], "lw%d" % wsi)
                W_("tensor", "lw%d" % wsi, wp_ld)
                W_("tensor", "ve", ve_hsp[hc_])
                for a in range(NSUB):
                    for dh in range(2):
                        for ps in range(3):
                            hv = (hh3, hh3, hl3)[ps]
                            wof = (dh * 512, D + dh * 512, dh * 512)[ps]
                            OP("tensor", lambda e, hc=hc_, a=a, hv=hv, wof=wof,
                               wsi=wsi, st=(hc_ == 0 and ps == 0): e.matmul(
                                out=p_big[a][:, (wof % D) // 512 * 512:
                                             (wof % D) // 512 * 512 + 512],
                                lhsT=hv[:, hc % DC, a * 128:a * 128 + 128],
                                rhs=wsb[wsi][:, wof:wof + 512],
                                start=st, stop=False))
                pe_n = PEG()
                bst["wq_rel"][wsi] = pe_n
                if hc_ + DC < HC:
                    emit_hsplit(hc_ + DC, pe_n)
            W_("tensor", "ac", ac_l2)
            for a in range(NSUB):
                for dh in range(2):
                    OP("tensor", lambda e, a=a, dh=dh: e.matmul(
                        out=p_big[a][:, dh * 512:dh * 512 + 512],
                        lhsT=lora2[:, a * 128:a * 128 + 128],
                        rhs=bprojT_sb[:, dh * 512:dh * 512 + 512],
                        start=False, stop=True))
            pe_y = PEG()

            W_("vector", "st", bst["st_done"])
            for a in range(NSUB):
                W_("vector", "pe", pe_y)
                OP("vector", lambda e, a=a: e.tensor_add(
                    out=t1_3[:, a, :], in0=p_big[a][:], in1=bproj_sb[:]))
                evac["p_big%d" % a] = ("ve", OP(
                    "vector", lambda e, a=a: e.tensor_add(
                        out=fin3[:, a, :], in0=t1_3[:, a, :], in1=x2v[:, a, :]),
                    ("ve", 1)))
            ve_fin = C["ve"]

            W_("gpsimd", "ve", ve_fin)
            DMA("gpsimd", y_out[tb * TAU:(tb + 1) * TAU, :].rearrange(
                "(s p) d -> p s d", p=128), h1s3, "st")
            bst["st_done"] = C["st"]
            bst["ve_end"] = ve_fin

        if phases >= 3:
            W_("gpsimd", "st", C["st"])
            W_("gpsimd", "gst", C["gst"])
            W_("gpsimd", "gstash", C["gstash"])
            W_("gpsimd", "gcc", C["gcc"])
            if dev:
                W_("gpsimd", "gdbg", C["gdbg"])

        @block.sync
        def _(eng):
            for fn in Q["sync"]:
                fn(eng)

        @block.vector
        def _(eng):
            for fn in Q["vector"]:
                fn(eng)

        @block.scalar
        def _(eng):
            for fn in Q["scalar"]:
                fn(eng)

        @block.tensor
        def _(eng):
            for fn in Q["tensor"]:
                fn(eng)

        @block.gpsimd
        def _(eng):
            for fn in Q["gpsimd"]:
                fn(eng)

    return nc


def make_inputs(full, core):
    seq, quarter = core // 4, core % 4
    x = np.ascontiguousarray(
        np.asarray(full["L0_attn_x"])[seq, quarter * TC:(quarter + 1) * TC, :])
    d = {"x": x}
    for n in ("1q", "2q", "1k", "2k", "1v", "2v"):
        d["w_" + n] = _slab_kxo(np.asarray(full[f"L0_attn_W{n}"]))
    for n in ("1q", "1k", "1v", "2q", "2k"):
        d["b_" + n] = _fm_bias(full[f"L0_attn_b{n}"])
    d["b2v_bc"] = np.broadcast_to(full["L0_attn_b2v"], (128, D))
    d["lnw_bc"] = np.broadcast_to(full["L0_attn_ln_w"], (128, D))
    d["lnb_bc"] = np.broadcast_to(full["L0_attn_ln_b"], (128, D))
    d["bproj_bc"] = np.broadcast_to(full["L0_ffn_b_proj"], (128, D))
    d["bfc_fm"] = _fm_bias(full["L0_ffn_b_fc"])
    wfcT = np.ascontiguousarray(np.asarray(full["L0_ffn_W_fc"]).T)
    d["wfcT_r"] = _slab_kxo(wfcT)
    wprojT = np.ascontiguousarray(np.asarray(full["L0_ffn_W_proj"]).T)
    d["wprojT_r"] = wprojT.reshape(HC, 128, D)
    afcT = np.asarray(full["L0_ffn_A_fc"]).T
    d["afcT"] = afcT.reshape(DC, 128, RK).transpose(1, 0, 2).reshape(128, DC * RK)
    d["bfcT"] = np.asarray(full["L0_ffn_B_fc"]).T
    aprojT = np.asarray(full["L0_ffn_A_proj"]).T
    d["aprojT"] = aprojT.reshape(HC, 128, RK).transpose(1, 0, 2).reshape(128, HC * RK)
    d["bprojT"] = np.asarray(full["L0_ffn_B_proj"]).T
    j = np.arange(128)[:, None]
    i = np.arange(TAU)[None, :]
    m = np.concatenate([(ac * 128 + j <= i).astype(np.float32)
                        for ac in range(NSUB)], axis=1)
    d["masks"] = m
    pm = np.zeros((128, NCORE), np.float32)
    for jc in range(NCORE):
        if jc // 4 == seq and jc % 4 < quarter:
            pm[:, jc] = 1.0
    d["pmask"] = pm
    d["ident"] = np.eye(128, dtype=np.float32)
    return {k: np.ascontiguousarray(v, dtype=np.float32) for k, v in d.items()}


def kernel(**inputs):
    nc = build_nc(dev=False)
    in_maps = [make_inputs(inputs, c) for c in range(NCORE)]
    res = run_bass_kernel_spmd(nc, in_maps, core_ids=list(range(NCORE)))
    out = np.empty((B, T, D), dtype=np.float32)
    for c in range(NCORE):
        seq, quarter = c // 4, c % 4
        out[seq, quarter * TC:(quarter + 1) * TC, :] = res.results[c]["y"]
    return out


# version 30 (final)
# speedup vs baseline: 1.0331x; 1.0100x over previous
import numpy as np
from contextlib import ExitStack

import concourse.bass as bass
import concourse.mybir as mybir
from concourse.bass_utils import run_bass_kernel_spmd

F32 = mybir.dt.float32
I32 = mybir.dt.int32
AF = mybir.ActivationFunctionType
ALU = mybir.AluOpType

B, T, D, HID, RK = 2, 8192, 1024, 4096, 16
NCORE = 8
TC = T * B // NCORE
TAU = 256
NT = TC // TAU
NSUB = TAU // 128
DC = D // 128
HC = HID // 128
EPS = 1e-5
INV_SCALE = float(1.0 / np.float64(np.float32(np.sqrt(np.float64(1024 * 2048)))))
INV_D = float(1.0 / 1024.0)


def _slab_kxo(w):
    K, O = w.shape
    a = np.ascontiguousarray(w).reshape(K // 128, 128, O // 128, 128)
    return np.ascontiguousarray(a.transpose(2, 1, 0, 3).reshape(O // 128, 128, K))


def _fm_bias(b):
    return np.ascontiguousarray(np.asarray(b).reshape(-1, 128).T)


def build_nc(dev=False, phases=3, qprep=True, nt=NT, upto=99):
    nc = bass.Bass("TRN2", target_bir_lowering=False, debug=False,
                   detect_race_conditions=False)

    def din(name, shape):
        return nc.dram_tensor(name, shape, F32, kind="ExternalInput").ap()

    x_in = din("x", [TC, D])
    w_attn = {n: din("w_" + n, [DC, 128, D]) for n in ("1q", "2q", "1k", "2k", "1v", "2v")}
    bias_fm = {n: din("b_" + n, [128, DC]) for n in ("1q", "1k", "1v", "2q", "2k")}
    b2v_bc = din("b2v_bc", [128, D])
    lnw_bc = din("lnw_bc", [128, D])
    lnb_bc = din("lnb_bc", [128, D])
    bproj_bc = din("bproj_bc", [128, D])
    bfc_fm = din("bfc_fm", [128, HC])
    wfcT_r = din("wfcT_r", [HC, 128, D])
    wprojT_r = din("wprojT_r", [HC, 128, D])
    afcT = din("afcT", [128, DC * RK])
    bfcT = din("bfcT", [RK, HID])
    aprojT = din("aprojT", [128, HC * RK])
    bprojT = din("bprojT", [RK, D])
    masks = din("masks", [128, NSUB * TAU])
    pmask = din("pmask", [128, NCORE])
    ident = din("ident", [128, 128])

    y_out = nc.dram_tensor("y", [TC, D], F32, kind="ExternalOutput").ap()

    ikind = "ExternalOutput" if dev else "Internal"
    q_st = nc.dram_tensor("q_st", [NT, 128, DC * TAU], F32, kind=ikind).ap()
    ol_st = nc.dram_tensor("ol_st", [NT, 128, NSUB * D], F32, kind=ikind).ap()
    BF16 = mybir.dt.bfloat16
    wfcq = nc.dram_tensor("wfcq", [HC, 128, 2 * D], BF16, kind=ikind).ap()
    wprojq = nc.dram_tensor("wprojq", [HC, 128, 2 * D], BF16, kind=ikind).ap()
    cc_in = nc.dram_tensor("cc_in", [128, DC * D], F32, kind="Internal").ap()
    cc_out = nc.dram_tensor("cc_out", [NCORE * 128, DC * D], F32, kind="Internal", addr_space="Shared").ap()
    scal_dr = nc.dram_tensor("scal_dr", [1, 8], F32, kind="Internal").ap()
    if dev:
        dbgS = nc.dram_tensor("dbgS", [128, DC * D], F32, kind="ExternalOutput").ap()
        dbgS0 = nc.dram_tensor("dbgS0", [128, DC * D], F32, kind="ExternalOutput").ap()
        dbgH = nc.dram_tensor("dbgH", [128, HC * TAU], F32, kind="ExternalOutput").ap()
        dbgX2 = nc.dram_tensor("dbgX2", [128, NSUB * D], F32, kind="ExternalOutput").ap()

    with ExitStack() as ctx:
        sb = lambda name, shape, dt=F32: ctx.enter_context(nc.sbuf_tensor(name, shape, dt))
        ident_sb = sb("ident_sb", [128, 128])
        masks_sb = sb("masks_sb", [128, NSUB * TAU])
        pmask_sb = sb("pmask_sb", [128, NCORE])
        bias_sb = {n: sb("bias_" + n, [128, DC]) for n in bias_fm}
        b2v_sb = sb("b2v_sb", [128, D])
        lnw_sb = sb("lnw_sb", [128, D])
        lnb_sb = sb("lnb_sb", [128, D])
        bproj_sb = sb("bproj_sb", [128, D])
        bfc_sb = sb("bfc_sb", [128, HC])
        afcT_sb = sb("afcT_sb", [128, DC * RK])

        aprojT_sb = sb("aprojT_sb", [128, HC * RK])
        bprojT_sb = sb("bprojT_sb", [RK, D])
        S_sb = sb("S_sb", [128, DC * D])
        stats = sb("stats", [128, 32])
        sq_sc = sb("sq_sc", [128, D])
        qmax = sb("qmax", [128, 4])
        sbc = sb("sbc", [128, 4])
        xt = sb("xt", [128, NSUB * D])
        xn_tm = sb("xn_tm", [128, NSUB * D])
        bigbuf = sb("bigbuf", [128, HC * TAU])
        h1 = sb("h1", [128, DC * TAU])
        qf = sb("qf", [128, DC * TAU])
        kf = sb("kf", [128, DC * TAU])
        ktm = sb("ktm", [128, NSUB * D])
        vtm = sb("vtm", [128, NSUB * D])
        ol = sb("ol", [128, NSUB * D])
        ws = [sb(f"ws{i}", [128, D]) for i in range(4)]
        wsb = [sb(f"wsb{i}", [128, 2 * D], mybir.dt.bfloat16) for i in range(2)]
        qi32 = sb("qi32", [128, D], I32)
        bfa = sb("bfa", [128, 2 * DC * TAU], mybir.dt.bfloat16)
        qout = [sb(f"qout{i}", [128, 2 * D], mybir.dt.bfloat16) for i in range(2)]
        lora1 = sb("lora1", [RK, TAU])
        lora2 = sb("lora2", [RK, TAU])
        xn_fm = bigbuf[:, 0:DC * TAU]
        qraw = [bigbuf[:, DC * TAU + 1024 * i:DC * TAU + 1024 * (i + 1)]
                for i in range(2)]
        qd = bigbuf[:, 4096:5120]
        qr = bigbuf[:, 5120:6144]
        sc_sb = bigbuf[:, 6144:6144 + NSUB * TAU]
        h_fm = bigbuf[:, 0:HC * TAU]

        ps = lambda name, shape: ctx.enter_context(nc.psum_tensor(name, shape, F32))
        p_tr = ps("p_tr", [128, 256])
        p_a = ps("p_a", [128, 256])
        p_b = ps("p_b", [128, 256])
        p_big = [ps("p_big0", [128, 1024]), ps("p_big1", [128, 1024])]
        p_l = ps("p_l", [128, 256])

        SEM = {}
        for s_ in ("ldc", "ldx", "ldb", "lw0", "lw1", "lw2", "lw3",
                   "gq0", "gq1", "gst", "gstash", "gcc", "gdbg",
                   "pe", "ve", "ac", "cc", "st"):
            SEM[s_] = ctx.enter_context(nc.semaphore("sem_" + s_))
        block = ctx.enter_context(nc.Block())

        Q = {"sync": [], "vector": [], "scalar": [], "tensor": [], "gpsimd": []}
        C = {k: 0 for k in SEM}

        def W_(eng, sem, val):
            if val is None or val <= 0:
                return
            Q[eng].append(lambda e, s=sem, v=val: e.wait_ge(SEM[s], v))

        def OP(eng, fn, inc=None):
            if inc is None:
                Q[eng].append(fn)
                return None
            sem, k = inc
            C[sem] += k
            Q[eng].append(lambda e, f=fn, s=sem, kk=k: f(e).then_inc(SEM[s], kk))
            return C[sem]

        def PEG():
            C["pe"] += 1
            last = Q["tensor"].pop()
            Q["tensor"].append(lambda e, f=last: f(e).then_inc(SEM["pe"], 1))
            return C["pe"]

        def VC(fn):
            n = OP("vector", fn, ("ve", 1))
            W_("vector", "ve", n)
            return n

        def DMA(eng, out_ap, in_ap, sem):
            return OP(eng, lambda e, o=out_ap, i=in_ap: e.dma_start(out=o, in_=i),
                      (sem, 16))

        evac = {"p_a": ("ve", 0), "p_b": ("ve", 0), "p_tr": ("ve", 0),
                "p_big0": ("ve", 0), "p_big1": ("ve", 0), "p_l": ("ac", 0)}

        def wait_bank(eng, name):
            s_, v = evac[name]
            W_(eng, s_, v)

        xt3 = xt[:].rearrange("p (s d) -> p s d", s=NSUB)
        xn3 = xn_tm[:].rearrange("p (s d) -> p s d", s=NSUB)
        xnf3 = xn_fm.rearrange("p (c t) -> p c t", c=DC)
        h13 = h1[:].rearrange("p (c t) -> p c t", c=DC)
        h1s3 = h1[:].rearrange("p (s d) -> p s d", s=NSUB)
        qf3 = qf[:].rearrange("p (c t) -> p c t", c=DC)
        kf3 = kf[:].rearrange("p (c t) -> p c t", c=DC)
        kfs3 = kf[:].rearrange("p (s d) -> p s d", s=NSUB)
        ktm3 = ktm[:].rearrange("p (a d) -> p a d", a=NSUB)
        vtm3 = vtm[:].rearrange("p (a d) -> p a d", a=NSUB)
        ol3 = ol[:].rearrange("p (a d) -> p a d", a=NSUB)
        S3 = S_sb[:].rearrange("p (c d) -> p c d", c=DC)
        xn2h3 = bfa[:, 0:DC * TAU].rearrange("p (c t) -> p c t", c=DC)
        xn2l3 = bfa[:, DC * TAU:2 * DC * TAU].rearrange("p (c t) -> p c t", c=DC)
        hf3 = h_fm.rearrange("p (c t) -> p c t", c=HC)
        hh3 = xn2h3
        hl3 = xn2l3
        bfcT_sb = ktm[0:RK, 0:NSUB * D]
        msk3 = masks_sb[:].rearrange("p (a t) -> p a t", a=NSUB)

        DMA("sync", ident_sb[:], ident, "ldc")
        DMA("sync", masks_sb[:], masks, "ldc")
        DMA("sync", pmask_sb[:], pmask, "ldc")
        for n in bias_sb:
            DMA("sync", bias_sb[n][:], bias_fm[n], "ldc")
        DMA("sync", b2v_sb[:], b2v_bc, "ldc")
        DMA("sync", lnw_sb[:], lnw_bc, "ldc")
        DMA("sync", lnb_sb[:], lnb_bc, "ldc")
        DMA("sync", bproj_sb[:], bproj_bc, "ldc")
        DMA("sync", bfc_sb[:], bfc_fm, "ldc")
        DMA("sync", afcT_sb[:], afcT, "ldc")
        DMA("sync", aprojT_sb[:], aprojT, "ldc")
        DMA("sync", bprojT_sb[:], bprojT, "ldc")
        ld_consts = C["ldc"]
        OP("vector", lambda e: e.memset(S_sb[:], 0.0))
        ve_S_ready = OP("vector", lambda e: e.memset(qmax[:, 0:2], 0.0), ("ve", 1))

        def q_slab_in(i):
            return wfcT_r[i] if i < HC else wprojT_r[i - HC]

        def q_slab_out(i):
            return wfcq[i] if i < HC else wprojq[i - HC]

        qstate = {"ve_prev": 0, "gd_load": [0] * 64, "ac9": 0, "st_gd": 0,
                  "st_prev": 0, "sbc_gd": 0, "sq_guard": 0}

        def emit_qmax_slab(i):
            buf = qraw[i % 2]
            W_("gpsimd", "ve", qstate["ve_prev"])
            qstate["gd_load"][i] = DMA("gpsimd", buf, q_slab_in(i), "gq%d" % (i % 2))
            W_("vector", "gq%d" % (i % 2), qstate["gd_load"][i])
            if i == 0:
                W_("vector", "ve", ve_S_ready)
            m = 0 if i < HC else 1
            VC(lambda e, b=buf: e.reduce_max(
                out=qmax[:, 2:3], in_=b, axis=mybir.AxisListType.X,
                apply_absolute_value=True))
            qstate["ve_prev"] = VC(lambda e, m=m: e.tensor_max(
                out=qmax[:, m:m + 1], in0=qmax[:, m:m + 1], in1=qmax[:, 2:3]))

        def emit_qscalars():
            for m in range(2):
                W_("vector", "gcc", C["gcc"])
                W_("vector", "ac", C["ac"])
                VC(lambda e: e.memset(sq_sc[:, 0:128], 0.0))
                vq = VC(lambda e, m=m: e.tensor_copy(
                    out=sq_sc[:, 0:1], in_=qmax[:, m:m + 1]))
                W_("tensor", "ve", vq)
                W_("tensor", "ldc", ld_consts)
                pe_n = OP("tensor", lambda e: e.transpose(
                    out=p_tr[:, 0:128], in_=sq_sc[:, 0:128], identity=ident_sb[:]),
                    ("pe", 1))
                qstate["sq_guard"] = pe_n
                W_("vector", "pe", pe_n)
                VC(lambda e: e.reduce_max(
                    out=stats[0:1, 16:17], in_=p_tr[0:1, 0:128],
                    axis=mybir.AxisListType.X))
                VC(lambda e: e.memset(stats[0:1, 17:18], 127.0))
                VC(lambda e: e.reciprocal(
                    out=stats[0:1, 18:19], in_=stats[0:1, 17:18]))
                VC(lambda e: e.tensor_mul(
                    out=stats[0:1, 19:20], in0=stats[0:1, 16:17],
                    in1=stats[0:1, 18:19]))
                vq2 = VC(lambda e: e.reciprocal(
                    out=stats[0:1, 20:21], in_=stats[0:1, 19:20]))
                W_("gpsimd", "ve", vq2)
                DMA("gpsimd", scal_dr[0:1, 2 * m:2 * m + 1], stats[0:1, 19:20], "gcc")
                g1 = DMA("gpsimd", scal_dr[0:1, 2 * m + 1:2 * m + 2],
                         stats[0:1, 20:21], "gcc")
                W_("gpsimd", "gcc", g1)
                DMA("gpsimd", sbc[:, 2 * m:2 * m + 1],
                    scal_dr[0:1, 2 * m:2 * m + 1].broadcast_to([128, 1]), "gcc")
                qstate["sbc_gd"] = DMA(
                    "gpsimd", sbc[:, 2 * m + 1:2 * m + 2],
                    scal_dr[0:1, 2 * m + 1:2 * m + 2].broadcast_to([128, 1]), "gcc")

        def emit_qquant_slab(i):
            buf = qraw[i % 2]
            qrx = qr
            m = 0 if i < HC else 1
            W_("gpsimd", "ve", qstate["ve_prev"])
            ld_i = DMA("gpsimd", buf, q_slab_in(i), "gq%d" % (i % 2))
            W_("vector", "gq%d" % (i % 2), ld_i)
            W_("vector", "gcc", qstate["sbc_gd"])
            W_("vector", "gst", qstate["st_prev"])
            OP("vector", lambda e, b=buf, m=m: e.tensor_scalar(
                out=qd, in0=b, scalar1=sbc[:, 2 * m + 1:2 * m + 2], scalar2=None,
                op0=ALU.mult))
            OP("vector", lambda e: e.tensor_copy(out=qi32[:], in_=qd))
            OP("vector", lambda e, q=qrx: e.tensor_copy(out=q, in_=qi32[:]))
            OP("vector", lambda e, b=buf, q=qrx: e.tensor_sub(out=b, in0=qd, in1=q))
            OP("vector", lambda e, b=buf, q=qrx: e.scalar_tensor_tensor(
                out=qd, in0=b, scalar=0.5, in1=q,
                op0=ALU.is_ge, op1=ALU.add))
            OP("vector", lambda e, b=buf, q=qrx: e.tensor_scalar(
                out=q, in0=b, scalar1=-0.5, scalar2=None, op0=ALU.is_lt))
            OP("vector", lambda e, b=buf, q=qrx: e.tensor_sub(out=b, in0=qd, in1=q))
            OP("vector", lambda e, b=buf: e.tensor_scalar(
                out=qd, in0=b, scalar1=-128.0, scalar2=127.0,
                op0=ALU.max, op1=ALU.min))
            qo = qout[i % 2]
            OP("vector", lambda e, m=m: e.tensor_scalar(
                out=qr, in0=qd, scalar1=sbc[:, 2 * m:2 * m + 1], scalar2=None,
                op0=ALU.mult))
            OP("vector", lambda e, q=qo: e.tensor_copy(out=q[:, 0:D], in_=qr))
            qstate["ve_prev"] = OP("vector", lambda e, q=qo: e.tensor_sub(
                out=q[:, D:2 * D], in0=qr, in1=q[:, 0:D]), ("ve", 1))
            W_("gpsimd", "ve", qstate["ve_prev"])
            qstate["st_prev"] = qstate["st_gd"]
            qstate["st_gd"] = DMA("gpsimd", q_slab_out(i), qo[:], "gst")

        qsched = {t: [] for t in range(NT)}
        for i in range(64 if qprep else 0):
            qsched[i // 32].append(("max", i))
        quant_per_tile = [11, 11, 11, 11, 10, 10] if qprep else []
        qi = 0
        for ti, n_ in enumerate(quant_per_tile):
            for _ in range(n_):
                qsched[2 + ti].append(("quant", qi))
                qi += 1
        assert qi == 64 or not qprep

        wnames = ["1q", "2q", "1k", "2k", "1v", "2v"]
        ws_release = [0, 0, 0, 0]
        ws_loaded = {}
        prev = {"ve_ln": 0, "ve_S": ve_S_ready, "gd_stash": 0}

        def tile_rows(t):
            return x_in[t * TAU:(t + 1) * TAU, :].rearrange("(s p) d -> p s d", p=128)

        ph1 = {}
        ph_ln = {}

        def emit_ln(t):
            W_("sync", "ve", prev["ve_ln"])
            ld_x = DMA("sync", xt3, tile_rows(t), "ldx")
            ph_ln[(t, "ldx")] = ld_x
            W_("vector", "gstash", prev["gd_stash"])
            W_("scalar", "gstash", prev["gd_stash"])
            W_("scalar", "pe", qstate["sq_guard"])
            for s_ in range(NSUB):
                W_("vector", "ldx", ld_x)
                VC(lambda e, s=s_: e.reduce_sum(
                    out=stats[:, 4 * s:4 * s + 1], in_=xt3[:, s, :],
                    axis=mybir.AxisListType.X))
                W_("scalar", "ldx", ld_x)
                ac_sq = OP("scalar", lambda e, s=s_: e.activation(
                    out=sq_sc[:], in_=xt3[:, s, :], func=AF.Square,
                    accum_out=stats[:, 4 * s + 1:4 * s + 2]), ("ac", 1))
                W_("vector", "ac", ac_sq)
                VC(lambda e, s=s_: e.tensor_scalar(
                    out=stats[:, 4 * s:4 * s + 1], in0=stats[:, 4 * s:4 * s + 1],
                    scalar1=INV_D, scalar2=None, op0=ALU.mult))
                VC(lambda e, s=s_: e.tensor_scalar(
                    out=stats[:, 4 * s + 1:4 * s + 2],
                    in0=stats[:, 4 * s + 1:4 * s + 2],
                    scalar1=INV_D, scalar2=None, op0=ALU.mult))
                VC(lambda e, s=s_: e.tensor_mul(
                    out=stats[:, 4 * s + 2:4 * s + 3], in0=stats[:, 4 * s:4 * s + 1],
                    in1=stats[:, 4 * s:4 * s + 1]))
                vv = VC(lambda e, s=s_: e.scalar_tensor_tensor(
                    out=stats[:, 4 * s + 2:4 * s + 3],
                    in0=stats[:, 4 * s + 1:4 * s + 2], scalar=EPS,
                    in1=stats[:, 4 * s + 2:4 * s + 3],
                    op0=ALU.add, op1=ALU.subtract))
                W_("scalar", "ve", vv)
                ac_sd = OP("scalar", lambda e, s=s_: e.activation(
                    out=stats[:, 4 * s + 3:4 * s + 4],
                    in_=stats[:, 4 * s + 2:4 * s + 3], func=AF.Sqrt), ("ac", 1))
                W_("vector", "ac", ac_sd)
                VC(lambda e, s=s_: e.reciprocal(
                    out=stats[:, 4 * s + 2:4 * s + 3],
                    in_=stats[:, 4 * s + 3:4 * s + 4]))
                OP("vector", lambda e, s=s_: e.tensor_scalar(
                    out=xn3[:, s, :], in0=xt3[:, s, :],
                    scalar1=stats[:, 4 * s:4 * s + 1],
                    scalar2=stats[:, 4 * s + 2:4 * s + 3],
                    op0=ALU.subtract, op1=ALU.mult), ("ve", 1))
            prev["ve_ln"] = C["ve"]
            ph_ln[t] = C["ve"]

        def emit_tr(t):
            ve_xn = ph_ln[t]
            ld_x = ph_ln[(t, "ldx")]
            for dc in range(DC):
                pb, pname = ((p_a, "p_a"), (p_b, "p_b"))[dc % 2]
                W_("tensor", "ve", ve_xn)
                wait_bank("tensor", pname)
                W_("tensor", "ldc", ld_consts)
                for s_ in range(NSUB):
                    OP("tensor", lambda e, s=s_, dc=dc, pb=pb: e.transpose(
                        out=pb[:, s * 128:s * 128 + 128],
                        in_=xn3[:, s, dc * 128:dc * 128 + 128],
                        identity=ident_sb[:]))
                pe_n = PEG()
                W_("vector", "pe", pe_n)
                evac[pname] = ("ve", OP("vector", lambda e, dc=dc, pb=pb: e.tensor_copy(
                    out=xnf3[:, dc, :], in_=pb[:, 0:TAU]), ("ve", 1)))
            ph1[t] = (ld_x, C["ve"])

        def emit_phase2(t, mid_hook=None):
            ld_x, ve_xnf = ph1[t]
            wsi_cnt = {"i": 0}

            def next_wsi():
                v = wsi_cnt["i"] % 4
                wsi_cnt["i"] += 1
                return v

            def load_wslab(mat, slab, wsi):
                W_("sync", "pe", ws_release[wsi])
                ws_loaded[(t, mat, slab)] = (
                    "lw%d" % wsi,
                    DMA("sync", ws[wsi][:], w_attn[mat][slab], "lw%d" % wsi))

            def mlp_h1(mat, bias_name):
                for hc_ in range(DC):
                    wsi = next_wsi()
                    load_wslab(mat, hc_, wsi)
                    pb, pname = ((p_a, "p_a"), (p_b, "p_b"))[hc_ % 2]
                    W_("tensor", *ws_loaded[(t, mat, hc_)])
                    W_("tensor", "ve", ve_xnf)
                    wait_bank("tensor", pname)
                    for kc in range(DC):
                        OP("tensor", lambda e, kc=kc, pb=pb, wsi=wsi: e.matmul(
                            out=pb[:, 0:TAU],
                            lhsT=ws[wsi][:, kc * 128:kc * 128 + 128],
                            rhs=xnf3[:, kc, :],
                            start=(kc == 0), stop=(kc == DC - 1)))
                    pe_n = PEG()
                    ws_release[wsi] = pe_n
                    W_("scalar", "pe", pe_n)
                    evac[pname] = ("ac", OP("scalar", lambda e, hc=hc_, pb=pb,
                                            bn=bias_name: e.activation(
                        out=h13[:, hc, :], in_=pb[:, 0:TAU], func=AF.Gelu,
                        bias=bias_sb[bn][:, hc:hc + 1]), ("ac", 1)))
                return C["ac"]

            def mlp_out_fm(mat, bias_name, dst3, ac_h1):
                last = 0
                for oc in range(DC):
                    wsi = next_wsi()
                    load_wslab(mat, oc, wsi)
                    pb, pname = ((p_a, "p_a"), (p_b, "p_b"))[oc % 2]
                    W_("tensor", *ws_loaded[(t, mat, oc)])
                    W_("tensor", "ac", ac_h1)
                    wait_bank("tensor", pname)
                    for hc_ in range(DC):
                        OP("tensor", lambda e, hc=hc_, pb=pb, wsi=wsi: e.matmul(
                            out=pb[:, 0:TAU],
                            lhsT=ws[wsi][:, hc * 128:hc * 128 + 128],
                            rhs=h13[:, hc, :],
                            start=(hc == 0), stop=(hc == DC - 1)))
                    pe_n = PEG()
                    ws_release[wsi] = pe_n
                    W_("scalar", "pe", pe_n)
                    last = OP("scalar", lambda e, oc=oc, pb=pb, bn=bias_name,
                              d3=dst3: e.activation(
                        out=d3[:, oc, :], in_=pb[:, 0:TAU], func=AF.Identity,
                        bias=bias_sb[bn][:, oc:oc + 1]), ("ac", 1))
                    evac[pname] = ("ac", last)
                return last

            def mlp_out_vtm(ac_h1):
                wait_bank("tensor", "p_big0")
                wait_bank("tensor", "p_big1")
                W_("tensor", "ac", ac_h1)
                pe_n = 0
                for oc in range(DC):
                    wsi = next_wsi()
                    load_wslab("2v", oc, wsi)
                    W_("tensor", *ws_loaded[(t, "2v", oc)])
                    for a in range(NSUB):
                        for hc_ in range(DC):
                            OP("tensor", lambda e, hc=hc_, oc=oc, a=a, wsi=wsi:
                               e.matmul(
                                out=p_big[a][:, oc * 128:oc * 128 + 128],
                                lhsT=h13[:, hc, a * 128:a * 128 + 128],
                                rhs=ws[wsi][:, hc * 128:hc * 128 + 128],
                                start=(hc == 0), stop=(hc == DC - 1)))
                    pe_n = PEG()
                    ws_release[wsi] = pe_n
                for a in range(NSUB):
                    W_("vector", "pe", pe_n)
                    evac["p_big%d" % a] = ("ve", OP(
                        "vector", lambda e, a=a: e.tensor_add(
                            out=vtm3[:, a, :], in0=p_big[a][:], in1=b2v_sb[:]),
                        ("ve", 1)))

            if upto < 3:
                return
            ac_h1q = mlp_h1("1q", "1q")
            if mid_hook is not None:
                mid_hook("a")
            qf_done = mlp_out_fm("2q", "2q", qf3, ac_h1q)
            if mid_hook is not None:
                mid_hook("b")
            if upto < 4:
                return
            ac_h1k = mlp_h1("1k", "1k")
            if mid_hook is not None:
                mid_hook("c")
            kf_done = mlp_out_fm("2k", "2k", kf3, ac_h1k)
            if mid_hook is not None:
                mid_hook("d")
            if upto < 5:
                return
            ac_h1v = mlp_h1("1v", "1v")
            if mid_hook is not None:
                mid_hook("e")
            mlp_out_vtm(ac_h1v)
            ve_vtm = C["ve"]

            if upto < 6:
                return
            for a in range(NSUB):
                pbn = "p_big%d" % (a % 2)
                W_("tensor", "ac", kf_done)
                wait_bank("tensor", pbn)
                for dc in range(DC):
                    OP("tensor", lambda e, dc=dc, a=a: e.transpose(
                        out=p_big[a % 2][:, dc * 128:dc * 128 + 128],
                        in_=kf3[:, dc, a * 128:a * 128 + 128],
                        identity=ident_sb[:]))
                pe_n = PEG()
                W_("vector", "pe", pe_n)
                evac[pbn] = ("ve", OP("vector", lambda e, a=a: e.tensor_copy(
                    out=ktm3[:, a, :], in_=p_big[a % 2][:]), ("ve", 1)))
            ve_ktm = C["ve"]

            if upto < 7:
                return
            for ac_ in range(NSUB):
                pb, pname = ((p_a, "p_a"), (p_b, "p_b"))[ac_ % 2]
                W_("tensor", "ac", max(qf_done, kf_done))
                wait_bank("tensor", pname)
                for kc in range(DC):
                    OP("tensor", lambda e, kc=kc, ac=ac_, pb=pb: e.matmul(
                        out=pb[:, 0:TAU],
                        lhsT=kf3[:, kc, ac * 128:ac * 128 + 128],
                        rhs=qf3[:, kc, :],
                        start=(kc == 0), stop=(kc == DC - 1)))
                pe_n = PEG()
                W_("vector", "pe", pe_n)
                evac[pname] = ("ve", OP("vector", lambda e, ac=ac_, pb=pb:
                                        e.scalar_tensor_tensor(
                    out=sc_sb[:, ac * TAU:(ac + 1) * TAU], in0=pb[:, 0:TAU],
                    scalar=INV_SCALE, in1=msk3[:, ac, :],
                    op0=ALU.mult, op1=ALU.mult), ("ve", 1)))
            ve_sT = C["ve"]

            if upto < 8:
                return
            for a in range(NSUB):
                pbn = "p_big%d" % a
                W_("tensor", "ve", max(ve_sT, ve_vtm, prev["ve_S"]))
                wait_bank("tensor", pbn)
                for d1c in range(DC):
                    for dh in range(2):
                        OP("tensor", lambda e, d1c=d1c, dh=dh, a=a: e.matmul(
                            out=p_big[a][:, dh * 512:dh * 512 + 512],
                            lhsT=qf3[:, d1c, a * 128:a * 128 + 128],
                            rhs=S3[:, d1c, dh * 512:dh * 512 + 512],
                            start=(d1c == 0), stop=False))
                for ac_ in range(NSUB):
                    for dh in range(2):
                        OP("tensor", lambda e, ac=ac_, dh=dh, a=a: e.matmul(
                            out=p_big[a][:, dh * 512:dh * 512 + 512],
                            lhsT=sc_sb[:, ac * TAU + a * 128:ac * TAU + a * 128 + 128],
                            rhs=vtm3[:, ac, dh * 512:dh * 512 + 512],
                            start=False, stop=(ac == NSUB - 1)))
                pe_n = PEG()
                W_("vector", "pe", pe_n)
                evac[pbn] = ("ve", OP("vector", lambda e, a=a: e.tensor_copy(
                    out=ol3[:, a, :], in_=p_big[a][:]), ("ve", 1)))
            ve_ol = C["ve"]
            pe_qS_done = C["pe"]

            if upto < 9:
                return
            for d1c in range(DC):
                pbn = "p_big%d" % (d1c % 2)
                wait_bank("tensor", pbn)
                W_("tensor", "ve", ve_ktm)
                for a in range(NSUB):
                    for dh in range(2):
                        OP("tensor", lambda e, a=a, dh=dh, d1c=d1c: e.matmul(
                            out=p_big[d1c % 2][:, dh * 512:dh * 512 + 512],
                            lhsT=ktm3[:, a, d1c * 128:d1c * 128 + 128],
                            rhs=vtm3[:, a, dh * 512:dh * 512 + 512],
                            start=(a == 0), stop=(a == NSUB - 1)))
                pe_n = PEG()
                W_("vector", "pe", pe_n)
                evac[pbn] = ("ve", OP("vector", lambda e, d1c=d1c: e.scalar_tensor_tensor(
                    out=S3[:, d1c, :], in0=p_big[d1c % 2][:], scalar=INV_SCALE,
                    in1=S3[:, d1c, :], op0=ALU.mult, op1=ALU.add), ("ve", 1)))
            prev["ve_S"] = C["ve"]

            if upto < 10:
                return
            W_("gpsimd", "gstash", prev["gd_stash"])
            W_("gpsimd", "ve", ve_ol)
            DMA("gpsimd", ol_st[t], ol[:], "gstash")
            W_("gpsimd", "ac", qf_done)
            W_("gpsimd", "pe", pe_qS_done)
            DMA("gpsimd", q_st[t], qf[:], "gstash")
            prev["gd_stash"] = C["gstash"]

            return

        def hook_fn(t, stage):
            if stage == "b":
                if t + 1 < nt:
                    emit_ln(t + 1)
                return
            items = qsched.get(t, [])
            k4 = (len(items) + 3) // 4 if items else 0
            idx = {"a": 0, "c": 1, "d": 2, "e": 3}[stage]
            for kind, i in items[idx * k4:(idx + 1) * k4]:
                if kind == "max":
                    emit_qmax_slab(i)
                else:
                    emit_qquant_slab(i)

        emit_ln(0)
        emit_tr(0)
        for t in range(nt):
            emit_phase2(t, lambda stage, tt=t: hook_fn(tt, stage))
            if t + 1 < nt:
                emit_tr(t + 1)
            if t == 1 and qprep:
                emit_qscalars()

        pe_sweepA = C["pe"]

        if phases == 1:
            if dev:
                W_("gpsimd", "ve", prev["ve_S"])
                DMA("gpsimd", dbgS, S_sb[:], "gdbg")
                W_("gpsimd", "gdbg", C["gdbg"])
            W_("gpsimd", "gstash", C["gstash"])
            W_("gpsimd", "gst", C["gst"])
        phase_c_on = phases >= 2
        ve_S0 = 0
        if phase_c_on:
          W_("gpsimd", "ve", prev["ve_S"])
          gcc_ccin = DMA("gpsimd", cc_in, S_sb[:], "gcc")
          W_("gpsimd", "gcc", C["gcc"])
          OP("gpsimd", lambda e: e.collective_compute(
              "AllGather", ALU.bypass,
              replica_groups=[[0, 1, 2, 3, 4, 5, 6, 7]],
              ins=[cc_in], outs=[cc_out]), ("cc", 1))
          W_("gpsimd", "cc", C["cc"])
          if dev:
              DMA("gpsimd", dbgS, S_sb[:], "gdbg")
          W_("gpsimd", "pe", pe_sweepA)
          W_("gpsimd", "gstash", C["gstash"])
          W_("gpsimd", "gst", C["gst"])
          W_("vector", "gcc", gcc_ccin)
          OP("vector", lambda e: e.memset(S_sb[:], 0.0), ("ve", 1))
          W_("sync", "pe", pe_sweepA)
          W_("sync", "gst", C["gst"])
          W_("sync", "gstash", C["gstash"])
          W_("sync", "cc", C["cc"])
          HB = DC * D // 2
          ve_half = {0: C["ve"], 1: C["ve"]}
          for j in range(NCORE):
              for h in range(2):
                  qn, sn = (("gpsimd", "gcc"), ("sync", "ldx"))[h]
                  W_(qn, "ve", ve_half[h])
                  gd_j = DMA(qn, bigbuf[:, h * HB:(h + 1) * HB],
                             cc_out[j * 128:(j + 1) * 128, h * HB:(h + 1) * HB],
                             sn)
                  W_("vector", sn, gd_j)
                  ve_half[h] = OP("vector", lambda e, j=j, h=h: e.scalar_tensor_tensor(
                      out=S_sb[:, h * HB:(h + 1) * HB],
                      in0=bigbuf[:, h * HB:(h + 1) * HB],
                      scalar=pmask_sb[:, j:j + 1],
                      in1=S_sb[:, h * HB:(h + 1) * HB],
                      op0=ALU.mult, op1=ALU.add), ("ve", 1))
          ve_S0 = C["ve"]
          if dev:
              W_("gpsimd", "ve", ve_S0)
              DMA("gpsimd", dbgS0, S_sb[:], "gdbg")
          if phases == 2:
              W_("gpsimd", "gcc", C["gcc"])
              W_("gpsimd", "gst", C["gst"])
              W_("gpsimd", "gstash", C["gstash"])
              if dev:
                  W_("gpsimd", "gdbg", C["gdbg"])

        xb3, qb3, olb3, ofull3 = xt3, qf3, ol3, xn3
        x2v = kfs3
        xn2tm3 = vtm3
        xn2fm3 = qf3
        hat3 = h1s3
        t1_3 = ktm3
        fin3 = h1s3
        def stp(base, k):
            return stats[:, base:base + 8].rearrange(
                "p (s c) -> p s c", c=4)[:, :, k:k + 1]

        def emit_ln1_stats():
            o = 8
            for s_ in range(NSUB):
                VC(lambda e, s=s_, o=o: e.reduce_sum(
                    out=stats[:, o + 4 * s:o + 4 * s + 1], in_=ofull3[:, s, :],
                    axis=mybir.AxisListType.X))
            W_("scalar", "ve", C["ve"])
            ac_sq = 0
            for s_ in range(NSUB):
                ac_sq = OP("scalar", lambda e, s=s_, o=o: e.activation(
                    out=sq_sc[:], in_=ofull3[:, s, :], func=AF.Square,
                    accum_out=stats[:, o + 4 * s + 1:o + 4 * s + 2]), ("ac", 1))
            W_("vector", "ac", ac_sq)
            VC(lambda e, o=o: e.tensor_scalar(
                out=stp(o, 0), in0=stp(o, 0),
                scalar1=INV_D, scalar2=None, op0=ALU.mult))
            VC(lambda e, o=o: e.tensor_scalar(
                out=stp(o, 1), in0=stp(o, 1),
                scalar1=INV_D, scalar2=None, op0=ALU.mult))
            VC(lambda e, o=o: e.tensor_mul(
                out=stp(o, 2), in0=stp(o, 0), in1=stp(o, 0)))
            vv = VC(lambda e, o=o: e.scalar_tensor_tensor(
                out=stp(o, 2), in0=stp(o, 1), scalar=EPS, in1=stp(o, 2),
                op0=ALU.add, op1=ALU.subtract))
            W_("scalar", "ve", vv)
            ac_sd = OP("scalar", lambda e, o=o: e.activation(
                out=stp(o, 3), in_=stp(o, 2), func=AF.Sqrt), ("ac", 1))
            W_("vector", "ac", ac_sd)
            VC(lambda e, o=o: e.reciprocal(out=stp(o, 2), in_=stp(o, 3)))

        bst = {"st_done": 0, "ve_end": ve_S0,
               "wq_rel": [pe_sweepA, pe_sweepA], "ac_gelu": [0] * HC,
               "lora_free": ("ac", 0)}

        for tb in range(NT if phases >= 3 else 0):
            W_("sync", "ve", bst["ve_end"])
            W_("sync", "st", bst["st_done"])
            if tb == 0:
                W_("sync", "gstash", C["gstash"])
                W_("sync", "gst", C["gst"])
                W_("sync", "cc", C["cc"])
                DMA("sync", xb3, tile_rows(tb), "ldb")
                DMA("sync", qf[:], q_st[tb], "ldb")
                ld_b = DMA("sync", ol[:], ol_st[tb], "ldb")
            else:
                ld_b = bst["ld_next"]

            if tb == 0:
                for a in range(NSUB):
                    pbn = "p_big%d" % a
                    W_("tensor", "ldb", ld_b)
                    W_("tensor", "ve", ve_S0)
                    wait_bank("tensor", pbn)
                    for d1c in range(DC):
                        for dh in range(2):
                            OP("tensor", lambda e, d1c=d1c, dh=dh, a=a: e.matmul(
                                out=p_big[a][:, dh * 512:dh * 512 + 512],
                                lhsT=qb3[:, d1c, a * 128:a * 128 + 128],
                                rhs=S3[:, d1c, dh * 512:dh * 512 + 512],
                                start=(d1c == 0), stop=(d1c == DC - 1)))
                    pe_n = PEG()
                    W_("vector", "pe", pe_n)
                    W_("vector", "ldb", ld_b)
                    evac[pbn] = ("ve", OP("vector", lambda e, a=a: e.tensor_add(
                        out=ofull3[:, a, :], in0=p_big[a][:], in1=olb3[:, a, :]),
                        ("ve", 1)))

            if tb == 0:
                emit_ln1_stats()
                o = 8
                for s_ in range(NSUB):
                    OP("vector", lambda e, s=s_, o=o: e.tensor_scalar(
                        out=xn2tm3[:, s, :], in0=ofull3[:, s, :],
                        scalar1=stats[:, o + 4 * s:o + 4 * s + 1],
                        scalar2=stats[:, o + 4 * s + 2:o + 4 * s + 3],
                        op0=ALU.subtract, op1=ALU.mult))
                    OP("vector", lambda e, s=s_: e.tensor_mul(
                        out=xn2tm3[:, s, :], in0=xn2tm3[:, s, :],
                        in1=lnw_sb[:, :]))
                    OP("vector", lambda e, s=s_: e.tensor_add(
                        out=xn2tm3[:, s, :], in0=xn2tm3[:, s, :],
                        in1=lnb_sb[:, :]))
            for s_ in range(NSUB):
                W_("vector", "ldb", ld_b)
                OP("vector", lambda e, s=s_: e.tensor_add(
                    out=x2v[:, s, :], in0=xn2tm3[:, s, :], in1=xb3[:, s, :]),
                    ("ve", 1))
            o = 16
            for s_ in range(NSUB):
                VC(lambda e, s=s_, o=o: e.reduce_sum(
                    out=stats[:, o + 4 * s:o + 4 * s + 1], in_=x2v[:, s, :],
                    axis=mybir.AxisListType.X))
            W_("scalar", "ve", C["ve"])
            ac_sq = 0
            for s_ in range(NSUB):
                ac_sq = OP("scalar", lambda e, s=s_, o=o: e.activation(
                    out=sq_sc[:], in_=x2v[:, s, :], func=AF.Square,
                    accum_out=stats[:, o + 4 * s + 1:o + 4 * s + 2]), ("ac", 1))
            W_("vector", "ac", ac_sq)
            VC(lambda e, o=o: e.tensor_scalar(
                out=stp(o, 0), in0=stp(o, 0),
                scalar1=INV_D, scalar2=None, op0=ALU.mult))
            VC(lambda e, o=o: e.tensor_scalar(
                out=stp(o, 1), in0=stp(o, 1),
                scalar1=INV_D, scalar2=None, op0=ALU.mult))
            VC(lambda e, o=o: e.tensor_mul(
                out=stp(o, 2), in0=stp(o, 0), in1=stp(o, 0)))
            vv = VC(lambda e, o=o: e.scalar_tensor_tensor(
                out=stp(o, 2), in0=stp(o, 1), scalar=EPS, in1=stp(o, 2),
                op0=ALU.add, op1=ALU.subtract))
            W_("scalar", "ve", vv)
            ac_sd = OP("scalar", lambda e, o=o: e.activation(
                out=stp(o, 3), in_=stp(o, 2), func=AF.Sqrt), ("ac", 1))
            W_("vector", "ac", ac_sd)
            VC(lambda e, o=o: e.reciprocal(out=stp(o, 2), in_=stp(o, 3)))
            for s_ in range(NSUB):
                OP("vector", lambda e, s=s_, o=o: e.tensor_scalar(
                    out=xn2tm3[:, s, :], in0=x2v[:, s, :],
                    scalar1=stats[:, o + 4 * s:o + 4 * s + 1],
                    scalar2=stats[:, o + 4 * s + 2:o + 4 * s + 3],
                    op0=ALU.subtract, op1=ALU.mult), ("ve", 1))
            ve_xn2 = C["ve"]
            if dev and tb == 0:
                W_("gpsimd", "ve", ve_xn2)
                DMA("gpsimd", dbgX2, kf[:], "gdbg")

            for dc in range(DC):
                pb, pname = ((p_a, "p_a"), (p_b, "p_b"))[dc % 2]
                W_("tensor", "ve", ve_xn2)
                wait_bank("tensor", pname)
                for s_ in range(NSUB):
                    OP("tensor", lambda e, s=s_, dc=dc, pb=pb: e.transpose(
                        out=pb[:, s * 128:s * 128 + 128],
                        in_=xn2tm3[:, s, dc * 128:dc * 128 + 128],
                        identity=ident_sb[:]))
                pe_n = PEG()
                W_("scalar", "pe", pe_n)
                ac_hi = OP("scalar", lambda e, dc=dc, pb=pb: e.activation(
                    out=xn2h3[:, dc, :], in_=pb[:, 0:TAU], func=AF.Copy), ("ac", 1))
                W_("vector", "pe", pe_n)
                W_("vector", "ac", ac_hi)
                OP("vector", lambda e, dc=dc, pb=pb: e.tensor_copy(
                    out=xn2fm3[:, dc, :], in_=pb[:, 0:TAU]))
                evac[pname] = ("ve", OP("vector", lambda e, dc=dc, pb=pb:
                                        e.tensor_sub(
                    out=xn2l3[:, dc, :], in0=pb[:, 0:TAU],
                    in1=xn2h3[:, dc, :]), ("ve", 1)))
            ve_xn2f = C["ve"]

            bst["pe_b6start"] = 0
            W_("tensor", "ve", ve_xn2f)
            W_("tensor", "ldc", ld_consts)
            wait_bank("tensor", "p_l")
            for dc in range(DC):
                OP("tensor", lambda e, dc=dc: e.matmul(
                    out=p_l[0:RK, 0:TAU],
                    lhsT=afcT_sb[:, dc * RK:(dc + 1) * RK],
                    rhs=xn2fm3[:, dc, :],
                    start=(dc == 0), stop=(dc == DC - 1)))
            pe_aproj = PEG()
            W_("scalar", "pe", pe_aproj)
            ac_l1 = OP("scalar", lambda e: e.activation(
                out=lora1[:], in_=p_l[0:RK, 0:TAU], func=AF.Copy, scale=2.0),
                ("ac", 1))
            evac["p_l"] = ("ac", ac_l1)

            for hc_ in range(HC):
                if hc_ % 16 == 0:
                    if hc_ == 0:
                        W_("sync", "ve", ve_xn2f)
                    else:
                        W_("sync", "pe", C["pe"])
                    bst["ld_bfc"] = DMA(
                        "sync", bfcT_sb,
                        bfcT[:, (hc_ // 16) * NSUB * D:(hc_ // 16 + 1) * NSUB * D],
                        "ldb")
                wsi = hc_ % 2
                W_("sync", "pe", bst["wq_rel"][wsi])
                wq_ld = DMA("sync", wsb[wsi][:], wfcq[hc_], "lw%d" % wsi)
                pb, pname = ((p_a, "p_a"), (p_b, "p_b"))[hc_ % 2]
                W_("tensor", "lw%d" % wsi, wq_ld)
                W_("tensor", "ldb", bst["ld_bfc"])
                W_("tensor", "ac", ac_l1)
                if hc_ == 8 and tb + 1 < NT:
                    W_("sync", "ve", ve_xn2f)
                    W_("sync", "pe", pe_aproj)
                    DMA("sync", xb3, tile_rows(tb + 1), "ldb")
                    DMA("sync", qf[:], q_st[tb + 1], "ldb")
                    bst["ld_next"] = DMA("sync", ol[:], ol_st[tb + 1], "ldb")
                wait_bank("tensor", pname)
                for dc in range(DC):
                    for ps in range(3):
                        wof = (dc * 128, dc * 128, D + dc * 128)[ps]
                        xv = (xn2h3, xn2l3, xn2h3)[ps]
                        OP("tensor", lambda e, dc=dc, pb=pb, wsi=wsi, wof=wof,
                           xv=xv, st=(dc == 0 and ps == 0): e.matmul(
                            out=pb[:, 0:TAU],
                            lhsT=wsb[wsi][:, wof:wof + 128],
                            rhs=xv[:, dc, :], start=st, stop=False))
                OP("tensor", lambda e, hc=hc_, pb=pb: e.matmul(
                    out=pb[:, 0:TAU],
                    lhsT=bfcT_sb[:, (hc % 16) * 128:(hc % 16) * 128 + 128],
                    rhs=lora1[:],
                    start=False, stop=True))
                pe_n = PEG()
                bst["wq_rel"][wsi] = pe_n
                W_("scalar", "pe", pe_n)
                ac_g = OP("scalar", lambda e, hc=hc_, pb=pb: e.activation(
                    out=hf3[:, hc, :], in_=pb[:, 0:TAU], func=AF.Gelu,
                    bias=bfc_sb[:, hc:hc + 1]), ("ac", 1))
                evac[pname] = ("ac", ac_g)
                bst["ac_gelu"][hc_] = ac_g
            ac_h = C["ac"]
            pe_b6_end = C["pe"]
            if dev and tb == 0:
                W_("gpsimd", "ac", ac_h)
                DMA("gpsimd", dbgH, h_fm, "gdbg")

            W_("tensor", "ac", ac_h)
            wait_bank("tensor", "p_l")
            for hc_ in range(HC):
                OP("tensor", lambda e, hc=hc_: e.matmul(
                    out=p_l[0:RK, 0:TAU],
                    lhsT=aprojT_sb[:, hc * RK:(hc + 1) * RK],
                    rhs=hf3[:, hc, :],
                    start=(hc == 0), stop=(hc == HC - 1)))
            pe_n = PEG()
            W_("scalar", "pe", pe_n)
            ac_l2 = OP("scalar", lambda e: e.activation(
                out=lora2[:], in_=p_l[0:RK, 0:TAU], func=AF.Copy, scale=2.0),
                ("ac", 1))
            evac["p_l"] = ("ac", ac_l2)

            wait_bank("tensor", "p_big0")
            wait_bank("tensor", "p_big1")
            ve_hsp = [0] * HC

            def emit_hsplit(hc_, pe_gate):
                W_("vector", "ac", bst["ac_gelu"][hc_])
                W_("vector", "pe", pe_gate)
                OP("vector", lambda e, hc=hc_: e.tensor_copy(
                    out=hh3[:, hc % DC, :], in_=hf3[:, hc, :]))
                ve_hsp[hc_] = OP("vector", lambda e, hc=hc_: e.tensor_sub(
                    out=hl3[:, hc % DC, :], in0=hf3[:, hc, :],
                    in1=hh3[:, hc % DC, :]), ("ve", 1))

            for hc_ in range(DC):
                emit_hsplit(hc_, pe_b6_end)
            for hc_ in range(HC):
                if hc_ == 24:
                    for s_ in range(NSUB):
                        OP("vector", lambda e, s=s_: e.tensor_add(
                            out=x2v[:, s, :], in0=x2v[:, s, :],
                            in1=bproj_sb[:]))
                if tb + 1 < NT and hc_ == 24:
                    emit_ln1_stats()
                    for s_ in range(NSUB):
                        OP("vector", lambda e, s=s_: e.tensor_scalar(
                            out=xn2tm3[:, s, :], in0=ofull3[:, s, :],
                            scalar1=stats[:, 8 + 4 * s:8 + 4 * s + 1],
                            scalar2=stats[:, 8 + 4 * s + 2:8 + 4 * s + 3],
                            op0=ALU.subtract, op1=ALU.mult))
                        OP("vector", lambda e, s=s_: e.tensor_mul(
                            out=xn2tm3[:, s, :], in0=xn2tm3[:, s, :],
                            in1=lnw_sb[:, :]))
                        OP("vector", lambda e, s=s_: e.tensor_add(
                            out=xn2tm3[:, s, :], in0=xn2tm3[:, s, :],
                            in1=lnb_sb[:, :]))
                if tb + 1 < NT and hc_ in (8, 10, 12, 14, 16, 18, 20, 22):
                    g = (hc_ - 8) // 2
                    a_, ch = g // 4, g % 4
                    pq, pqn = ((p_a, "p_a"), (p_b, "p_b"))[g % 2]
                    W_("tensor", "ldb", bst["ld_next"])
                    W_("tensor", "ve", ve_S0)
                    wait_bank("tensor", pqn)
                    for d1c in range(DC):
                        OP("tensor", lambda e, d1c=d1c, a=a_, ch=ch, pq=pq:
                           e.matmul(
                            out=pq[:, 0:256],
                            lhsT=qb3[:, d1c, a * 128:a * 128 + 128],
                            rhs=S3[:, d1c, ch * 256:ch * 256 + 256],
                            start=(d1c == 0), stop=(d1c == DC - 1)))
                    pe_g = PEG()
                    W_("vector", "pe", pe_g)
                    W_("vector", "ldb", bst["ld_next"])
                    evac[pqn] = ("ve", OP("vector", lambda e, a=a_, ch=ch,
                                          pq=pq: e.tensor_add(
                        out=ofull3[:, a, ch * 256:ch * 256 + 256],
                        in0=pq[:, 0:256],
                        in1=olb3[:, a, ch * 256:ch * 256 + 256]), ("ve", 1)))
                wsi = hc_ % 2
                W_("sync", "pe", bst["wq_rel"][wsi])
                wp_ld = DMA("sync", wsb[wsi][:], wprojq[hc_], "lw%d" % wsi)
                W_("tensor", "lw%d" % wsi, wp_ld)
                W_("tensor", "ve", ve_hsp[hc_])
                for a in range(NSUB):
                    for dh in range(2):
                        for ps in range(3):
                            hv = (hh3, hh3, hl3)[ps]
                            wof = (dh * 512, D + dh * 512, dh * 512)[ps]
                            OP("tensor", lambda e, hc=hc_, a=a, hv=hv, wof=wof,
                               wsi=wsi, st=(hc_ == 0 and ps == 0): e.matmul(
                                out=p_big[a][:, (wof % D) // 512 * 512:
                                             (wof % D) // 512 * 512 + 512],
                                lhsT=hv[:, hc % DC, a * 128:a * 128 + 128],
                                rhs=wsb[wsi][:, wof:wof + 512],
                                start=st, stop=False))
                pe_n = PEG()
                bst["wq_rel"][wsi] = pe_n
                if hc_ + DC < HC:
                    emit_hsplit(hc_ + DC, pe_n)
            W_("tensor", "ac", ac_l2)
            for a in range(NSUB):
                for dh in range(2):
                    OP("tensor", lambda e, a=a, dh=dh: e.matmul(
                        out=p_big[a][:, dh * 512:dh * 512 + 512],
                        lhsT=lora2[:, a * 128:a * 128 + 128],
                        rhs=bprojT_sb[:, dh * 512:dh * 512 + 512],
                        start=False, stop=True))
            pe_y = PEG()

            W_("vector", "st", bst["st_done"])
            for a in range(NSUB):
                W_("vector", "pe", pe_y)
                evac["p_big%d" % a] = ("ve", OP(
                    "vector", lambda e, a=a: e.tensor_add(
                        out=fin3[:, a, :], in0=p_big[a][:], in1=x2v[:, a, :]),
                    ("ve", 1)))
            ve_fin = C["ve"]

            W_("gpsimd", "ve", ve_fin)
            DMA("gpsimd", y_out[tb * TAU:(tb + 1) * TAU, :].rearrange(
                "(s p) d -> p s d", p=128), h1s3, "st")
            bst["st_done"] = C["st"]
            bst["ve_end"] = ve_fin

        if phases >= 3:
            W_("gpsimd", "st", C["st"])
            W_("gpsimd", "gst", C["gst"])
            W_("gpsimd", "gstash", C["gstash"])
            W_("gpsimd", "gcc", C["gcc"])
            if dev:
                W_("gpsimd", "gdbg", C["gdbg"])

        @block.sync
        def _(eng):
            for fn in Q["sync"]:
                fn(eng)

        @block.vector
        def _(eng):
            for fn in Q["vector"]:
                fn(eng)

        @block.scalar
        def _(eng):
            for fn in Q["scalar"]:
                fn(eng)

        @block.tensor
        def _(eng):
            for fn in Q["tensor"]:
                fn(eng)

        @block.gpsimd
        def _(eng):
            for fn in Q["gpsimd"]:
                fn(eng)

    return nc


def make_inputs(full, core):
    seq, quarter = core // 4, core % 4
    x = np.ascontiguousarray(
        np.asarray(full["L0_attn_x"])[seq, quarter * TC:(quarter + 1) * TC, :])
    d = {"x": x}
    for n in ("1q", "2q", "1k", "2k", "1v", "2v"):
        d["w_" + n] = _slab_kxo(np.asarray(full[f"L0_attn_W{n}"]))
    for n in ("1q", "1k", "1v", "2q", "2k"):
        d["b_" + n] = _fm_bias(full[f"L0_attn_b{n}"])
    d["b2v_bc"] = np.broadcast_to(full["L0_attn_b2v"], (128, D))
    d["lnw_bc"] = np.broadcast_to(full["L0_attn_ln_w"], (128, D))
    d["lnb_bc"] = np.broadcast_to(full["L0_attn_ln_b"], (128, D))
    d["bproj_bc"] = np.broadcast_to(full["L0_ffn_b_proj"], (128, D))
    d["bfc_fm"] = _fm_bias(full["L0_ffn_b_fc"])
    wfcT = np.ascontiguousarray(np.asarray(full["L0_ffn_W_fc"]).T)
    d["wfcT_r"] = _slab_kxo(wfcT)
    wprojT = np.ascontiguousarray(np.asarray(full["L0_ffn_W_proj"]).T)
    d["wprojT_r"] = wprojT.reshape(HC, 128, D)
    afcT = np.asarray(full["L0_ffn_A_fc"]).T
    d["afcT"] = afcT.reshape(DC, 128, RK).transpose(1, 0, 2).reshape(128, DC * RK)
    d["bfcT"] = np.asarray(full["L0_ffn_B_fc"]).T
    aprojT = np.asarray(full["L0_ffn_A_proj"]).T
    d["aprojT"] = aprojT.reshape(HC, 128, RK).transpose(1, 0, 2).reshape(128, HC * RK)
    d["bprojT"] = np.asarray(full["L0_ffn_B_proj"]).T
    j = np.arange(128)[:, None]
    i = np.arange(TAU)[None, :]
    m = np.concatenate([(ac * 128 + j <= i).astype(np.float32)
                        for ac in range(NSUB)], axis=1)
    d["masks"] = m
    pm = np.zeros((128, NCORE), np.float32)
    for jc in range(NCORE):
        if jc // 4 == seq and jc % 4 < quarter:
            pm[:, jc] = 1.0
    d["pmask"] = pm
    d["ident"] = np.eye(128, dtype=np.float32)
    return {k: np.ascontiguousarray(v, dtype=np.float32) for k, v in d.items()}


def kernel(**inputs):
    nc = build_nc(dev=False)
    in_maps = [make_inputs(inputs, c) for c in range(NCORE)]
    res = run_bass_kernel_spmd(nc, in_maps, core_ids=list(range(NCORE)))
    out = np.empty((B, T, D), dtype=np.float32)
    for c in range(NCORE):
        seq, quarter = c // 4, c % 4
        out[seq, quarter * TC:(quarter + 1) * TC, :] = res.results[c]["y"]
    return out
